# version 10
# baseline (speedup 1.0000x reference)
import math
from collections import defaultdict

import numpy as np
import ml_dtypes

import concourse.bass as bass
import concourse.mybir as mybir
from concourse.bass_utils import run_bass_kernel_spmd

F32 = mybir.dt.float32
BF16 = mybir.dt.bfloat16
AF = mybir.ActivationFunctionType
ALU = mybir.AluOpType
AX = mybir.AxisListType

NCORES = 8
B = 4
T = 2048
D = 1024
H = 16
HD = 64
CH = 128
NCH = T // CH
KVH = 4
E0 = 8
H0 = 4096
E1 = 8
I1 = 512
NEG = -1.0e9

BF = ml_dtypes.bfloat16


class CallTok:
    __slots__ = ("meth", "a", "k", "incs")

    def __init__(self, meth, a, k):
        self.meth = meth
        self.a = a
        self.k = k
        self.incs = []

    def then_inc(self, sem_name, n):
        self.incs.append((sem_name, n))
        return self


class Rec:
    def __init__(self):
        self.calls = []

    def emit(self, meth, *a, **k):
        tok = CallTok(meth, a, k)
        self.calls.append(tok)
        return tok


ENGS = ("tensor", "vector", "scalar", "gpsimd", "sync")
SEMS = ("tv", "vv", "sv", "gq", "sq")
OWN = {"tensor": "tv", "vector": "vv", "scalar": "sv"}


class Em:

    def __init__(self):
        self.rec = {e: Rec() for e in ENGS}
        self.cnt = defaultdict(int)
        self.last_wait = {e: defaultdict(int) for e in ENGS}
        self._pend = {}

    def op(self, eng, meth, *a, **k):
        if eng in ("vector", "scalar"):
            pend = self._pend.get(eng)
            if pend is not None:
                s, v = pend
                if self.last_wait[eng][s] < v:
                    self.rec[eng].emit("wait_ge", s, v)
                    self.last_wait[eng][s] = v
                self._pend[eng] = None
        tok = self.rec[eng].emit(meth, *a, **k)
        if eng in ("vector", "scalar") and meth != "wait_ge":
            out_ap = k.get("out")
            if out_ap is None and a and hasattr(a[0], "free_size"):
                out_ap = a[0]
            small = False
            if out_ap is not None:
                try:
                    small = out_ap.free_size() <= 16
                except Exception:
                    small = False
            if small:
                ev = self.inc(eng, tok, OWN[eng], 1)
                self._pend[eng] = ev
        return tok

    def inc(self, eng, tok, sem, n=1):
        if tok.incs:
            s0 = tok.incs[0][0]
            return (s0, self.cnt[s0])
        self.cnt[sem] += n
        tok.then_inc(sem, n)
        ev = (sem, self.cnt[sem])
        if eng == "tensor" and OWN.get(eng) == sem:
            self.last_wait[eng][sem] = ev[1]
        return ev

    def sig(self, eng, meth, *a, **k):
        tok = self.op(eng, meth, *a, **k)
        if tok.incs:
            return (tok.incs[0][0], self.cnt[tok.incs[0][0]])
        return self.inc(eng, tok, OWN[eng], 1)

    def w(self, eng, *evs):
        best = {}
        for ev in evs:
            if ev is None:
                continue
            s, v = ev
            if v > best.get(s, -1):
                best[s] = v
        for s, v in best.items():
            if self.last_wait[eng][s] >= v:
                continue
            self.rec[eng].emit("wait_ge", s, v)
            self.last_wait[eng][s] = v

    def dma(self, eng, out, in_, **k):
        sem = "gq" if eng == "gpsimd" else "sq"
        tok = self.op(eng, "dma_start", out=out, in_=in_, **k)
        return self.inc(eng, tok, sem, 16)

    def replay(self, nc, sems):
        def run(eng_obj, rec):
            for tok in rec.calls:
                if tok.meth == "wait_ge":
                    eng_obj.wait_ge(sems[tok.a[0]], tok.a[1])
                    continue
                ins = getattr(eng_obj, tok.meth)(*tok.a, **tok.k)
                for s, n in tok.incs:
                    ins.then_inc(sems[s], n)

        with nc.Block() as block:
            if self.rec["tensor"].calls:
                block.tensor(lambda t: run(t, self.rec["tensor"]))
            if self.rec["vector"].calls:
                block.vector(lambda v: run(v, self.rec["vector"]))
            if self.rec["scalar"].calls:
                block.scalar(lambda s: run(s, self.rec["scalar"]))
            if self.rec["gpsimd"].calls:
                block.gpsimd(lambda g: run(g, self.rec["gpsimd"]))
            if self.rec["sync"].calls:
                block.sync(lambda s: run(s, self.rec["sync"]))


class Prog:
    def __init__(self, debug=(), batches=None, phases=("A", "B", "C", "D", "E", "F", "G"),
                 b_nchunks=99, b_stage=99, b_nheads=99, b_nproj=99, b_bstep=99):
        self.b_nchunks = b_nchunks
        self.b_nproj = b_nproj
        self.b_bstep = b_bstep
        self.b_stage = b_stage
        self.b_nheads = b_nheads
        self.debug = set(debug)
        self.batches = batches if batches is not None else list(range(B))
        self.phases = set(phases)
        self.nc = bass.Bass("TRN2", target_bir_lowering=False,
                            detect_race_conditions=False)
        self.E = Em()
        self.dbg_outs = []
        self.dbg_n = 0
        self.region_ev = {}
        self._psum_bf = {}
        self._ctmp_user = {}
        self._qaug_user = {}
        self._rs_user2 = {}
        self._pt_user = {}
        self._wbuf_user = {}
        self._xt_user = {}
        self.declare_io()
        self.alloc_sbuf()
        self.alloc_psum()

    def declare_io(self):
        nc = self.nc
        d = {}

        def inp(name, shape, dt=F32):
            d[name] = nc.dram_tensor(name, list(shape), dt, kind="ExternalInput").ap()

        inp("x", (B * T, D))
        for w in ("wq", "wk", "wv", "wbeta", "walpha"):
            inp(w, (D, D))
        inp("bq_c", (128, 8))
        inp("balpha_c", (128, 8))
        inp("bk_r", (1, D))
        inp("bv_r", (1, D))
        inp("bbeta_r", (1, D))
        inp("w1", (E0, D, H0))
        inp("w2", (E0, H0, D))
        inp("b1c", (E0 * 4 * 128, 8))
        inp("b2r", (E0, D))
        inp("wr", (D, E0))
        inp("br_r", (1, E0))
        inp("l1wq", (D, D))
        inp("l1wk", (D, KVH * HD))
        inp("l1wv", (D, KVH * HD))
        inp("l1wo", (D, D))
        inp("wgT", (D, E1))
        inp("wfcT", (E1, D, I1))
        inp("wprojT", (E1, I1, D))
        inp("ident", (128, 128))
        inp("ones_col", (128, 1))
        inp("ones_row", (1, 128))
        inp("umat", (128, 128))
        inp("c1mat", (128, 128))
        inp("rconst", (128, H * CH))
        inp("hmap2", (128, 2))
        inp("rope_cos", (128, T))
        inp("rope_sin", (128, T))
        inp("mask_j", (128, 4 * 512))
        inp("mask_q", (128, 4 * 512))
        inp("rep4", (4, 16))
        self.inp = d

        self.out = nc.dram_tensor("out", [B * T, D], F32, kind="ExternalOutput").ap()
        self.y_add = nc.dram_tensor("y_add", [B * T, D], F32, kind="Internal").ap()
        self.x1_d = nc.dram_tensor("x1_d", [B * T, D], F32, kind="Internal").ap()
        self.y2_add = nc.dram_tensor("y2_add", [B * T, D], F32, kind="Internal").ap()
        self.scr_m = nc.dram_tensor("scr_m", [16, 128], F32, kind="Internal").ap()
        self.scr_b = nc.dram_tensor("scr_b", [16, 128], F32, kind="Internal").ap()
        self.scr_nb = nc.dram_tensor("scr_nb", [16, 2048], F32, kind="Internal").ap()

    def alloc_sbuf(self):
        nc = self.nc
        A = nc.alloc_sbuf_tensor
        self.A1 = A("A1", [128, 16384], BF16)
        self.A2 = A("A2", [128, 16384], BF16)
        self.A3 = A("A3", [128, 16384], BF16)
        self.A4 = A("A4", [128, 8192], BF16)
        self.A5 = A("A5", [128, 4352], BF16)
        self.A6 = A("A6", [128, 4096], BF16)
        self.WK = A("WK", [128, 6144], F32)
        self.c_ones_rowf = A("c_ones_rowf", [1, 128], F32)
        self.c_ident = A("c_ident", [128, 128], F32)
        self.c_identb = A("c_identb", [128, 128], BF16)
        self.c_ones_col = A("c_ones_col", [128, 1], F32)
        self.c_ones_rowb = A("c_ones_rowb", [1, 128], BF16)
        self.c_umat = A("c_umat", [128, 128], F32)
        self.c_c1 = A("c_c1", [128, 128], F32)
        self.c_rall = A("c_rall", [128, H * CH], F32)
        self.c_hmap2 = A("c_hmap2", [128, 2], BF16)
        self.c_cos = A("c_cos", [128, T], BF16)
        self.c_sin = A("c_sin", [128, T], BF16)
        self.c_mask_j = A("c_mask_j", [128, 4 * 512], BF16)
        self.c_rep4 = A("c_rep4", [4, 16], F32)
        self.bq8 = A("bq8_sb", [128, 8], F32)
        self.cbias = A("cbias", [128, 2], F32)
        self.balpha_c = A("balpha_sb", [128, 8], F32)
        self.biasrows = A("biasrows", [1, 3 * D], BF16)
        self.b2row = A("b2row", [1, D], BF16)
        self.brrow = A("brrow", [1, E0], BF16)
        self.b1c_buf = A("b1c_buf", [128, 8], F32)
        self.S_all = A("S_all", [128, 512], F32)
        self.S_bf = A("S_bf", [128, 512], BF16)
        self.bm_sb = A("bm_sb", [128, 8 * H], F32)
        self.lam_sb = A("lam_sb", [128, 16], F32)
        self.m_sb = A("m_sb", [128, 16], F32)
        self.negm_sb = A("negm_sb", [128, 16], F32)
        self.curve_sb = A("curve_sb", [128, 16], F32)
        self.mT_sb = A("mT_sb", [16, 128], F32)
        self.asum_row = A("asum_row", [1, 16], F32)
        self.asumbc = A("asumbc", [128, 8], F32)
        self.g_all = A("g_all", [128, 16 * 8], F32)
        self.wr_bf = A("wr_bf", [128, 64], BF16)
        self.nbq = A("nbq", [16, 2048], BF16)
        self.nbk = A("nbk", [4, 2048], BF16)
        self.k2m = A("k2m", [16, 1], F32)
        self.rs_row = A("rs_row", [1, 1024], F32)
        self.stat = A("stat", [128, 64], F32)
        self.statB = A("statB", [128, 64], F32)
        self.b_all = A("b_all", [128, 16], F32)
        self.expT = A("expT", [128, 2 * 128], BF16)
        self.attnT = A("attnT", [128, 2 * 128], BF16)
        self.tmp64 = A("tmp64", [128, 2 * 64], F32)
        self.ptbuf = A("ptbuf", [128, 4 * 512], BF16)

    def alloc_psum(self):
        nc = self.nc
        self.PS = [nc.alloc_psum_tensor(f"PS{i}", [128, 512], F32) for i in range(8)]

    def ps_bf(self, i):
        if i not in self._psum_bf:
            self._psum_bf[i] = self.PS[i].ap().bitcast(BF16)
        return self._psum_bf[i]

    def wk(self, i, w=1024):
        return self.WK.ap()[:, 1024 * i: 1024 * i + w]

    def dbg(self, name, ap, shape=None, wait=None):
        if name not in self.debug:
            return
        shape = shape or list(ap.shape)
        t = self.nc.dram_tensor(f"dbg_{name}", list(shape), F32,
                                kind="ExternalOutput").ap()
        eng = "sync" if ap.dtype == F32 else "gpsimd"
        if wait is not None:
            self.E.w(eng, wait)
        ev = self.E.dma(eng, t, ap)
        for e in ("tensor", "vector", "scalar", "gpsimd", "sync"):
            self.E.w(e, ev)
        self.dbg_outs.append(ev)
        return ev

    def accum_dma(self, dst_region_key, dst_ap, src_ap, first=False):
        E = self.E
        prev = self.region_ev.get(dst_region_key)
        if prev is not None:
            E.w("gpsimd", prev)
        if first:
            ev = E.dma("gpsimd", dst_ap, src_ap)
        else:
            ev = E.dma("gpsimd", dst_ap, src_ap, accum_op=ALU.add)
        self.region_ev[dst_region_key] = ev
        return ev

    def load_consts(self):
        E = self.E
        i = self.inp
        evs = []
        E.sig("vector", "memset", self.cbias.ap()[:, 0:1], 1.0)
        self.cb_ev = E.sig("vector", "memset", self.cbias.ap()[:, 1:2], 1.0e-5)
        for dst, src in (
            (self.c_ident, i["ident"]), (self.c_ones_col, i["ones_col"]),
            (self.c_umat, i["umat"]), (self.c_c1, i["c1mat"]),
            (self.c_rall, i["rconst"]), (self.c_ones_rowf, i["ones_row"]),
            (self.c_rep4, i["rep4"]),
            (self.bq8, i["bq_c"]), (self.balpha_c, i["balpha_c"]),
        ):
            evs.append(E.dma("sync", dst.ap(), src))
        for dst, src in (
            (self.c_identb, i["ident"]), (self.c_ones_rowb, i["ones_row"]),
            (self.c_hmap2, i["hmap2"]), (self.c_cos, i["rope_cos"]),
            (self.c_sin, i["rope_sin"]),
            (self.c_mask_j, i["mask_j"]),
            (self.biasrows.ap()[:, 0:D], i["bk_r"]),
            (self.biasrows.ap()[:, D:2 * D], i["bv_r"]),
            (self.biasrows.ap()[:, 2 * D:3 * D], i["bbeta_r"]),
            (self.brrow, i["br_r"]),
        ):
            evs.append(E.dma("gpsimd", dst if isinstance(dst, bass.AP) else dst.ap(), src))
        ev_c = evs[-1]
        E.w("scalar", *evs)
        ev = E.sig("scalar", "activation", out=self.bq8[:], in_=self.bq8[:],
                   func=AF.Copy, scale=0.125)
        self.consts_ev = evs + [ev]


    def phase_A(self, b, src="x", dst_xnT=None):
        if not hasattr(self, "_xb_user"):
            self._xb_user = []
        self.ln_groups(b, self.inp["x"], None, None, self.A1)
        self.dbg(f"xnT_{b}", self.A1.ap(), wait=self.xnT_ev[-1])

    def ln_tile(self, tc, xt, dstA, w_scalar=None):
        E = self.E
        if not hasattr(self, "_rs_user"):
            self._rs_user = {}
            self._xc_user = {}
        sl = tc % 2
        st = self.stat.ap()
        mu = st[:, 4 + sl:5 + sl]
        rs = st[:, 6 + sl:7 + sl]
        xc = self.wk(4 if sl == 0 else 5)
        xnb = xt.bitcast(BF16)[:, 0:1024]
        E.sig("vector", "reduce_sum", mu, xt, axis=AX.X)
        E.sig("vector", "tensor_scalar_mul", mu, mu, 1.0 / D)
        prev = self._xc_user.get(sl)
        if prev is not None:
            E.w("vector", prev)
        ev_xc = E.sig("vector", "tensor_scalar_sub", xc, xt, mu)
        E.w("scalar", ev_xc, w_scalar)
        prev = self._rs_user.get(sl)
        if prev is not None:
            E.w("scalar", prev)
        ev_ss = E.sig("scalar", "activation", out=xt, in_=xc, func=AF.Square,
                      accum_out=rs)
        self._xc_user[sl] = ev_ss
        E.sig("scalar", "activation", out=rs, in_=rs, func=AF.Ln,
              scale=1.0 / D, bias=self.cbias.ap()[:, 1:2])
        ev_rs = E.sig("scalar", "activation", out=rs, in_=rs, func=AF.Exp,
                      scale=-0.5)
        E.w("vector", ev_rs)
        ev_xn = E.sig("vector", "tensor_scalar_mul", xnb, xc, rs)
        self._rs_user[sl] = ev_xn
        bank = 6 + sl
        pbf = self.ps_bf(bank)
        self.ps_w(bank)
        E.w("tensor", ev_xn)
        for p in range(8):
            tok = E.op("tensor", "transpose", pbf[:, 128 * p:128 * p + 128],
                       xnb[:, 128 * p:128 * p + 128], self.c_identb[:])
        ev_tr = E.inc("tensor", tok, "tv", 1)
        self.ps_note_w(bank, ev_tr)
        self.ps_r("vector", bank)
        for p in range(8):
            tok = E.op("vector", "tensor_copy",
                       dstA.ap()[:, 2048 * p + 128 * tc: 2048 * p + 128 * tc + 128],
                       pbf[:, 128 * p:128 * p + 128])
        cev = E.inc("vector", tok, "vv", 1)
        self.ps_note_r(bank, cev)
        self.xnT_ev.append(cev)
        return ev_tr

    def ln_groups(self, b, x_src, y_src, x1_dst, dstA):
        E = self.E
        self.xnT_ev = []
        xb = self.WK.ap()[:, 0:4096]
        yb = self.A2.ap()[:, 0:8192].bitcast(F32)
        for tg in range(4):
            row0 = b * T + 512 * tg
            E.w("sync", *self._xb_user)
            self._xb_user = []
            ev_x = E.dma("sync", xb.rearrange("p (s c) -> p s c", c=1024),
                         x_src[row0:row0 + 512, :].rearrange(
                             "(s p) c -> p s c", p=128))
            sev = None
            if y_src is not None:
                pv = getattr(self, "_yb_user", None)
                if pv is not None:
                    E.w("sync", pv)
                ev_y = E.dma("sync", yb.rearrange("p (s c) -> p s c", c=1024),
                             y_src[row0:row0 + 512, :].rearrange(
                                 "(s p) c -> p s c", p=128))
                E.w("vector", ev_x, ev_y)
                aev = E.sig("vector", "tensor_add", xb, xb, yb)
                self._yb_user = aev
                E.w("sync", aev)
                sev = E.dma("sync", x1_dst[row0:row0 + 512, :].rearrange(
                    "(s p) c -> p s c", p=128),
                    xb.rearrange("p (s c) -> p s c", c=1024))
                self._xb_user.append(sev)
            else:
                E.w("vector", ev_x)
            for sub in range(4):
                tc = 4 * tg + sub
                xt = xb[:, 1024 * sub:1024 * sub + 1024]
                tr = self.ln_tile(tc, xt, dstA, w_scalar=sev)
            self._xb_user.append(tr)

    def ps_w(self, bank):
        for ev in getattr(self, "_ps_r", {}).get(bank, []):
            self.E.w("tensor", ev)

    def ps_note_w(self, bank, ev):
        if not hasattr(self, "_ps_wv"):
            self._ps_wv = {}
            self._ps_r = {}
        self._ps_wv[bank] = ev
        self._ps_r[bank] = []

    def ps_r(self, eng, bank):
        self.E.w(eng, self._ps_wv[bank])

    def ps_note_r(self, bank, ev):
        self._ps_r.setdefault(bank, []).append(ev)

    def join(self):
        E = self.E
        snap = [(s, E.cnt[s]) for s in SEMS if E.cnt[s] > 0]
        for e in ENGS:
            E.w(e, *snap)

    def phase_B(self, b):
        E = self.E
        self.join()
        ev1 = E.sig("vector", "memset", self.S_all[:], 0.0)
        self._sbf_ev = E.sig("vector", "memset", self.S_bf[:], 0.0)
        for hb in range(2):
            if self.b_stage < 1 and hb > 0:
                break
            self.b_proj(b, hb)
            if self.b_stage < 2:
                continue
            for ci in range(8):
                if 8 * hb + ci >= self.b_nchunks:
                    break
                self.b_chunk(b, hb, ci)

    def b_proj(self, b, hb):
        E = self.E
        A1, A2, A3 = self.A1.ap(), self.A2.ap(), self.A3.ap()
        A4, A5, A6 = self.A4.ap(), self.A5.ap(), self.A6.ap()
        xnT_ok = self.xnT_ev[-1]

        wnames = ("wq", "walpha", "wk", "wv", "wbeta")
        wbuf = {}
        wload = {}

        WKEY = {"wq": "A4", "walpha": "A56", "wk": "A4", "wv": "A56",
                "wbeta": "A56"}

        def load_w(i):
            wn = wnames[i]
            key = WKEY[wn]
            if key == "A4":
                views = [A4[:, 1024 * k:1024 * k + 1024] for k in range(8)]
            else:
                views = ([A5[:, 1024 * k:1024 * k + 1024] for k in range(4)]
                         + [A6[:, 1024 * k:1024 * k + 1024] for k in range(4)])
            prev = self._wbuf_user.get(key)
            if prev is not None:
                E.w("gpsimd", prev)
            src = self.inp[wn].rearrange("(k p) d -> p k d", p=128)
            if key == "A4":
                ev = E.dma("gpsimd", A4[:, 0:8192].rearrange("p (k d) -> p k d", d=1024), src)
            else:
                E.dma("gpsimd", A5[:, 0:4096].rearrange("p (k d) -> p k d", d=1024),
                      src[:, 0:4, :])
                ev = E.dma("gpsimd", A6[:, 0:4096].rearrange("p (k d) -> p k d", d=1024),
                          src[:, 4:8, :])
            wbuf[wn] = views
            wload[wn] = ev

        load_w(0)
        load_w(1)

        def lhs_w(wn, k, m):
            v = wbuf[wn][k]
            return v[:, 128 * m:128 * m + 128]

        def rhs_w(wn, k, n):
            v = wbuf[wn][k]
            return v[:, 512 * n:512 * n + 512]

        last_use = {}

        bank_rot = [0, 1, 2, 3]
        bi = 0
        for wi, (wn, feat) in enumerate((("wq", "q"), ("walpha", "a"))):
            if wi >= self.b_nproj:
                return
            E.w("tensor", wload[wn], xnT_ok)
            for m in range(8):
                for n in range(2):
                    bank = bank_rot[bi % 4]
                    bi += 1
                    ps = self.PS[bank].ap()
                    self.ps_w(bank)
                    for k in range(8):
                        tok = E.op("tensor", "matmul", ps[:, 0:512],
                                   lhs_w(wn, k, m),
                                   A1[:, 2048 * k + 1024 * hb + 512 * n:
                                      2048 * k + 1024 * hb + 512 * n + 512],
                                   start=(k == 0), stop=(k == 7))
                    ev = E.inc("tensor", tok, "tv", 1)
                    self.ps_note_w(bank, ev)
                    last_use[wn] = ev
                    if feat == "q":
                        self.ps_r("scalar", bank)
                        rev = E.sig("scalar", "activation",
                                    out=A2[:, 1024 * m + 512 * n:1024 * m + 512 * n + 512],
                                    in_=ps[:, 0:512], func=AF.Identity, scale=0.125,
                                    bias=self.bq8.ap()[:, m:m + 1])
                        self.ps_note_r(bank, rev)
                        self.qT_ev = rev
                    else:
                        wkA = self.wk(2)[:, 0:512] if n == 0 else self.wk(2)[:, 512:1024]
                        wkB = self.wk(3)[:, 0:512] if n == 0 else self.wk(3)[:, 512:1024]
                        self.ps_r("vector", bank)
                        E.sig("vector", "tensor_scalar", out=wkA, in0=ps[:, 0:512],
                              scalar1=self.balpha_c.ap()[:, m:m + 1], scalar2=10.0,
                              op0=ALU.add, op1=ALU.min)
                        rev = E.sig("vector", "tensor_scalar", out=wkA, in0=wkA,
                                    scalar1=-10.0, scalar2=None, op0=ALU.max)
                        self.ps_note_r(bank, rev)
                        E.w("scalar", rev)
                        aev0 = E.sig("scalar", "activation", out=wkB, in_=wkA,
                                     func=AF.Abs)
                        E.sig("scalar", "activation", out=wkB, in_=wkB, func=AF.Exp,
                              scale=-1.0)
                        sev = E.sig("scalar", "activation", out=wkB, in_=wkB,
                                    func=AF.Ln, bias=self.cbias.ap()[:, 0:1])
                        E.w("vector", aev0)
                        E.sig("vector", "tensor_scalar_min", wkA, wkA, 0.0)
                        E.w("vector", sev)
                        lev = E.sig("vector", "tensor_sub",
                                    A2[:, 8192 + 1024 * m + 512 * n:
                                       8192 + 1024 * m + 512 * n + 512],
                                    wkA, wkB)
                        self.laT_ev = lev
            self._wbuf_user[WKEY[wn]] = last_use[wn]
            load_w(wi + 2)

        bias_off = {"wk": 0, "wv": D, "wbeta": 2 * D}
        for wi, wn in ((2, "wk"), (3, "wv"), (4, "wbeta")):
            if wi >= self.b_nproj:
                return
            E.w("tensor", wload[wn], self.consts_ev[-1])
            if wn == "wbeta":
                E.w("vector", last_use["wk"])
            for ci in range(8):
                for n in range(2):
                    bank = bank_rot[bi % 4]
                    bi += 1
                    ps = self.PS[bank].ap()
                    self.ps_w(bank)
                    for k in range(8):
                        E.op("tensor", "matmul", ps[:, 0:512],
                             A1[:, 2048 * k + 1024 * hb + 128 * ci:
                                2048 * k + 1024 * hb + 128 * ci + 128],
                             rhs_w(wn, k, n),
                             start=(k == 0), stop=False)
                    tok = E.op("tensor", "matmul", ps[:, 0:512],
                               self.c_ones_rowb.ap()[0:1, :],
                               self.biasrows.ap()[0:1, bias_off[wn] + 512 * n:
                                                  bias_off[wn] + 512 * n + 512],
                               start=False, stop=True)
                    ev = E.inc("tensor", tok, "tv", 1)
                    self.ps_note_w(bank, ev)
                    last_use[wn] = ev
                    if wn == "wk":
                        self.ps_r("scalar", bank)
                        rev = E.sig("scalar", "activation",
                                    out=A3[:, 1024 * ci + 512 * n:1024 * ci + 512 * n + 512],
                                    in_=ps[:, 0:512], func=AF.Copy, scale=0.125)
                        self.ps_note_r(bank, rev)
                        self.k_ev = rev
                    elif wn == "wv":
                        self.ps_r("scalar", bank)
                        rev = E.sig("scalar", "activation",
                                    out=A3[:, 8192 + 1024 * ci + 512 * n:
                                           8192 + 1024 * ci + 512 * n + 512],
                                    in_=ps[:, 0:512], func=AF.Copy)
                        self.ps_note_r(bank, rev)
                        self.v_ev = rev
                    else:
                        wkA = self.wk(4)[:, 0:512] if n == 0 else self.wk(4)[:, 512:1024]
                        wkB = self.wk(5)[:, 0:512] if n == 0 else self.wk(5)[:, 512:1024]
                        if self.b_bstep < 0:
                            self.ps_note_r(bank, ev)
                            if not hasattr(self, "_beta_scr_ev"):
                                self._beta_scr_ev = {}
                            self._beta_scr_ev[n] = None
                            self.vb_ev = ev
                            continue
                        prev = getattr(self, "_beta_scr_ev", {}).get(n)
                        if prev is not None:
                            E.w("scalar", prev)
                        self.ps_r("scalar", bank)
                        abev0 = E.sig("scalar", "activation", out=wkB, in_=ps[:, 0:512],
                              func=AF.Abs)
                        self.ps_note_r(bank, abev0)
                        E.sig("scalar", "activation", out=wkB, in_=wkB, func=AF.Exp,
                              scale=-1.0)
                        sev = E.sig("scalar", "activation", out=wkB, in_=wkB,
                                    func=AF.Ln, bias=self.cbias.ap()[:, 0:1])
                        self.ps_r("vector", bank)
                        E.w("vector", abev0)
                        mev = E.sig("vector", "tensor_scalar_max", wkA,
                                    ps[:, 0:512], 0.0)
                        self.ps_note_r(bank, mev)
                        if self.b_bstep < 1:
                            self._beta_scr_ev[n] = mev
                            self.vb_ev = mev
                            continue
                        E.w("vector", sev)
                        bev = E.sig("vector", "tensor_add", wkA, wkA, wkB)
                        if not hasattr(self, "_beta_scr_ev"):
                            self._beta_scr_ev = {}
                        if self.b_bstep < 2:
                            self._beta_scr_ev[n] = bev
                            self.vb_ev = bev
                            continue
                        E.sig("vector", "reduce_sum",
                              self.bm_sb.ap()[:, 16 * ci + 8 * n:16 * ci + 8 * n + 8],
                              wkA.rearrange("p (h k) -> p h k", k=64), axis=AX.X)
                        if self.b_bstep < 3:
                            self._beta_scr_ev[n] = bev
                            self.vb_ev = bev
                            continue
                        bt = A4[:, 1024 * ci + 512 * n:1024 * ci + 512 * n + 512]
                        vev = E.sig("vector", "tensor_copy", bt, wkA)
                        self._beta_scr_ev[n] = vev
                        self.vb_ev = vev
            self._wbuf_user[WKEY[wn]] = last_use[wn]
            if wn == "wv":
                load_w(4)
        if hb == 0:
            self.dbg(f"qT_{b}", A2[:, 0:8192], wait=self.qT_ev)
            self.dbg(f"laT_{b}", A2[:, 8192:16384], wait=self.laT_ev)
            self.dbg(f"ktok_{b}", A3[:, 0:8192], wait=self.k_ev)
            self.dbg(f"vb_{b}", A3[:, 8192:16384], wait=self.vb_ev)
            self.dbg(f"bm_{b}", self.bm_sb.ap(), wait=self.vb_ev)

    def b_chunk(self, b, hb, ci):
        E = self.E
        A2, A3, A5 = self.A2.ap(), self.A3.ap(), self.A5.ap()
        ng = 8 * hb + ci
        t0 = 1024 * hb + 128 * ci
        PS = [p.ap() for p in self.PS]

        A4 = self.A4.ap()
        kch = A3[:, 1024 * ci:1024 * ci + 1024]
        vch = A3[:, 8192 + 1024 * ci:8192 + 1024 * ci + 1024]
        bch = A4[:, 1024 * ci:1024 * ci + 1024]
        kT = A5[:, 0:1024]
        kbm = A5[:, 1024:2048]
        vbch = A5[:, 2048:3072]

        E.w("tensor", self.laT_ev, self.vb_ev, self.k_ev, self.qT_ev,
            self._sbf_ev)

        self.ps_w(4)
        for k in range(8):
            tok = E.op("tensor", "matmul", PS[4][:, 2 * k:2 * k + 2],
                       A2[:, 8192 + 1024 * k + 128 * ci:8192 + 1024 * k + 128 * ci + 128],
                       self.c_hmap2.ap()[:],
                       start=(k == 0), stop=(k == 7))
        ev = E.inc("tensor", tok, "tv", 1)
        self.ps_note_w(4, ev)
        prev = getattr(self, "_lam_user", None)
        if prev is not None:
            E.w("vector", prev)
        self.ps_r("vector", 4)
        lam_ev = E.sig("vector", "tensor_copy", self.lam_sb[:], PS[4][:, 0:16])
        self.ps_note_r(4, lam_ev)
        self.ps_w(4)
        E.op("tensor", "matmul", PS[4][:, 16:32], self.c_umat[:], self.lam_sb[:],
             start=True, stop=True)
        tok = E.op("tensor", "matmul", PS[4][0:1, 32:48], self.c_ones_col[:],
                   self.lam_sb[:], start=True, stop=True)
        ev = E.inc("tensor", tok, "tv", 1)
        self.ps_note_w(4, ev)
        self._lam_user = ev
        for prev in getattr(self, "_m_user", []):
            E.w("vector", prev)
        self.ps_r("vector", 4)
        mev = E.sig("vector", "tensor_copy", self.m_sb[:], PS[4][:, 16:32])
        nev = E.sig("vector", "tensor_scalar_mul", self.negm_sb[:], self.m_sb[:], -1.0)
        E.w("scalar", mev)
        prev = getattr(self, "_curve_user", None)
        if prev is not None:
            E.w("scalar", prev)
        cev = E.sig("scalar", "activation", out=self.curve_sb[:], in_=self.m_sb[:],
                    func=AF.Exp)
        self.ps_r("scalar", 4)
        prev = getattr(self, "_asum_user", None)
        if prev is not None:
            E.w("scalar", prev)
        aev = E.sig("scalar", "activation", out=self.asum_row[:], in_=PS[4][0:1, 32:48],
                    func=AF.Exp)
        self.ps_note_r(4, aev)
        prev = getattr(self, "_mt_user", None)
        self.ps_w(4)
        E.w("tensor", mev)
        tok = E.op("tensor", "transpose", PS[4][0:16, 48:176], self.m_sb[:],
                   self.c_ident[:])
        ev = E.inc("tensor", tok, "tv", 1)
        self.ps_note_w(4, ev)
        self.ps_r("vector", 4)
        if prev is not None:
            E.w("vector", prev)
        mtev = E.sig("vector", "tensor_copy", self.mT_sb[:], PS[4][0:16, 48:176])
        self.ps_note_r(4, mtev)
        E.w("gpsimd", mtev)
        prevT1 = getattr(self, "_rall_user", None)
        ev1 = E.dma("gpsimd", self.scr_m, self.mT_sb[:])
        self._mt_user = ev1
        self._bT_user = ev1
        E.w("gpsimd", ev1)
        if prevT1 is not None:
            E.w("gpsimd", prevT1)
        rrow_ev = E.dma("gpsimd", self.c_rall.ap()[0:1, :],
                        self.scr_m.rearrange("h c -> (h c)").unsqueeze(0))
        self.ps_w(4)
        E.w("tensor", aev)
        asr = self.asum_row.ap().rearrange("p (h two) -> p two h", two=2)
        E.op("tensor", "matmul", PS[4][0:64, 176:184],
             self.c_ones_rowf.ap()[0:1, 0:64], asr[:, 0, :], start=True, stop=True)
        tok = E.op("tensor", "matmul", PS[4][64:128, 176:184],
                   self.c_ones_rowf.ap()[0:1, 0:64], asr[:, 1, :],
                   start=True, stop=True)
        ev = E.inc("tensor", tok, "tv", 1)
        self._asum_user = ev
        self.ps_note_w(4, ev)
        self.ps_r("vector", 4)
        abev = E.sig("vector", "tensor_copy", self.asumbc[:], PS[4][:, 176:184])
        self.ps_note_r(4, abev)
        self.ps_w(2)
        E.w("tensor", self.k_ev)
        prev_kt = getattr(self, "_kT_user", None)
        pbf = self.ps_bf(2)
        for p in range(8):
            tok = E.op("tensor", "transpose", pbf[:, 128 * p:128 * p + 128],
                       kch[:, 128 * p:128 * p + 128], self.c_identb[:])
        ev = E.inc("tensor", tok, "tv", 1)
        self.ps_note_w(2, ev)
        self.ps_r("vector", 2)
        if prev_kt is not None:
            E.w("vector", prev_kt)
        ktev = E.sig("vector", "tensor_copy", kT, pbf[:, 0:1024])
        self.ps_note_r(2, ktev)
        prev_kbm = getattr(self, "_kbm_user", None)
        if prev_kbm is not None:
            E.w("vector", prev_kbm)
        kbm_ev = None
        for h in range(H):
            tok = E.op("vector", "tensor_scalar",
                       out=kbm[:, 64 * h:64 * h + 64],
                       in0=kch[:, 64 * h:64 * h + 64],
                       scalar1=self.bm_sb.ap()[:, 16 * ci + h:16 * ci + h + 1],
                       scalar2=1.0 / 64, op0=ALU.mult, op1=ALU.mult)
        kbm_ev = E.inc("vector", tok, "vv", 1)
        prev_vb = getattr(self, "_vbch_user", None)
        if prev_vb is not None:
            E.w("vector", prev_vb)
        vbch_ev = E.sig("vector", "tensor_tensor", vbch, vch, bch, ALU.mult)

        if ng == 0:
            self.dbg("lam0", self.lam_sb.ap(), wait=lam_ev)
            self.dbg("m0", self.m_sb.ap(), wait=mev)
            self.dbg("curve0", self.curve_sb.ap(), wait=cev)
            self.dbg("asum0", self.asum_row.ap(), wait=aev)
            self.dbg("mT0", self.mT_sb.ap(), wait=mtev)
            self.dbg("asumbc0", self.asumbc.ap(), wait=abev)
            self.dbg("kT0", kT, wait=ktev)
            self.dbg("kbm0", kbm, wait=kbm_ev)
        if self.b_stage < 3:
            return
        if not hasattr(self, "_slot_user"):
            self._slot_user = {}
        wk_y = self.wk(ci % 2)
        prev_y_dma = getattr(self, "_ych_user", {}).get(ci % 2) or []
        y_evs = []
        mem_ev = None
        front = {}
        for h in range(min(H, self.b_nheads) + 1):
            if h < min(H, self.b_nheads):
                tb = (0, 1)[h % 2]
                sb = (2, 3)[h % 2]
                qslice = A2[64 * (h % 2):64 * (h % 2) + 64,
                            1024 * (h // 2) + 128 * ci:1024 * (h // 2) + 128 * ci + 128]
                self.ps_w(tb)
                E.w("tensor", rrow_ev)
                tok = E.op("tensor", "matmul", PS[tb][:, 0:128], self.c_c1[:],
                           self.c_rall.ap()[:, 128 * h:128 * h + 128],
                           start=True, stop=True)
                t1ev = E.inc("tensor", tok, "tv", 1)
                self.ps_note_w(tb, t1ev)
                self._rall_user = t1ev
                E.w("scalar", nev)
                self.ps_r("scalar", tb)
                prev = self._slot_user.get(("expT", h % 2))
                if prev is not None:
                    E.w("scalar", prev)
                eev = E.sig("scalar", "activation",
                            out=self.expT.ap()[:, 128 * (h % 2):128 * (h % 2) + 128],
                            in_=PS[tb][:, 0:128], func=AF.Exp,
                            bias=self.negm_sb.ap()[:, h:h + 1])
                self.ps_note_r(tb, eev)
                self.ps_w(sb)
                tok = E.op("tensor", "matmul", PS[sb][:, 0:128],
                           kT[64 * (h % 2):64 * (h % 2) + 64,
                              128 * (h // 2):128 * (h // 2) + 128],
                           qslice, start=True, stop=True)
                stev = E.inc("tensor", tok, "tv", 1)
                self.ps_note_w(sb, stev)
                E.w("vector", eev)
                self.ps_r("vector", sb)
                prev = self._slot_user.get(("attnT", h % 2))
                if prev is not None:
                    E.w("vector", prev)
                aev2 = E.sig("vector", "tensor_tensor",
                             self.attnT.ap()[:, 128 * (h % 2):128 * (h % 2) + 128],
                             PS[sb][:, 0:128],
                             self.expT.ap()[:, 128 * (h % 2):128 * (h % 2) + 128],
                             ALU.mult)
                self.ps_note_r(sb, aev2)
                self._slot_user[("expT", h % 2)] = aev2
                front[h] = (aev2, qslice)
            if h >= 1:
                hb2 = h - 1
                g = hb2 // 2
                yb = (5, 6)[hb2 % 2]
                db = (7, 4)[g % 2]
                aev2, qslice = front.pop(hb2)
                self.ps_w(yb)
                E.w("tensor", aev2, vbch_ev)
                E.op("tensor", "matmul", PS[yb][:, 0:64],
                     self.attnT.ap()[:, 128 * (hb2 % 2):128 * (hb2 % 2) + 128],
                     vbch[:, 64 * hb2:64 * hb2 + 64], start=True, stop=True)
                tok = E.op("tensor", "matmul", PS[yb][:, 64:128], qslice,
                           self.S_bf.ap()[64 * (hb2 % 2):64 * (hb2 % 2) + 64,
                                          64 * (hb2 // 2):64 * (hb2 // 2) + 64],
                           start=True, stop=True)
                ymev = E.inc("tensor", tok, "tv", 1)
                self.ps_note_w(yb, ymev)
                mem_ev = ymev
                self._slot_user[("attnT", hb2 % 2)] = ymev
                self.ps_r("vector", yb)
                E.w("vector", cev)
                if hb2 == 0:
                    E.w("vector", *prev_y_dma)
                E.sig("vector", "tensor_scalar_mul",
                      self.tmp64.ap()[:, 64 * (hb2 % 2):64 * (hb2 % 2) + 64],
                      PS[yb][:, 64:128], self.curve_sb.ap()[:, hb2:hb2 + 1])
                yev = E.sig("vector", "scalar_tensor_tensor",
                            out=wk_y[:, 64 * hb2:64 * hb2 + 64], in0=PS[yb][:, 0:64],
                            scalar=1.0,
                            in1=self.tmp64.ap()[:, 64 * (hb2 % 2):64 * (hb2 % 2) + 64],
                            op0=ALU.mult, op1=ALU.add)
                self.ps_note_r(yb, yev)
                y_evs.append(yev)
                if hb2 % 2 == 0:
                    self.ps_w(db)
                    E.w("tensor", kbm_ev, self.vb_ev)
                    E.op("tensor", "matmul", PS[db][0:64, 0:64],
                         kbm[:, 64 * hb2:64 * hb2 + 64], vch[:, 64 * hb2:64 * hb2 + 64],
                         start=True, stop=True)
                else:
                    tok = E.op("tensor", "matmul", PS[db][64:128, 0:64],
                               kbm[:, 64 * hb2:64 * hb2 + 64], vch[:, 64 * hb2:64 * hb2 + 64],
                               start=True, stop=True)
                    dev = E.inc("tensor", tok, "tv", 1)
                    self.ps_note_w(db, dev)
                    self.ps_r("vector", db)
                    E.w("vector", abev)
                    sev = E.sig("vector", "scalar_tensor_tensor",
                                out=self.S_all.ap()[:, 64 * g:64 * g + 64],
                                in0=self.S_all.ap()[:, 64 * g:64 * g + 64],
                                scalar=self.asumbc.ap()[:, g:g + 1],
                                in1=PS[db][0:128, 0:64],
                                op0=ALU.mult, op1=ALU.add)
                    self.ps_note_r(db, sev)
        self._kT_user = mem_ev
        self._kbm_user = ("tv", self.E.cnt["tv"])
        self._vbch_user = ("tv", self.E.cnt["tv"])
        self._wbuf_user["A4"] = ("tv", self.E.cnt["tv"])
        self._m_user = [mem_ev, ("sv", self.E.cnt["sv"])]
        self._curve_user = y_evs[-1]
        E.w("vector", mem_ev)
        sbf_ev = E.sig("vector", "tensor_copy", self.S_bf[:], self.S_all[:])
        self._sbf_ev = sbf_ev
        if ng <= 2:
            self.dbg(f"S{ng}", self.S_all.ap(), wait=sbf_ev)
        if ng == 1:
            self.dbg("lam1", self.lam_sb.ap(), wait=sbf_ev)
            self.dbg("curve1", self.curve_sb.ap(), wait=sbf_ev)
            self.dbg("asumbc1", self.asumbc.ap(), wait=sbf_ev)
        E.w("gpsimd", *y_evs[-1:])
        row0 = b * T + t0
        ydma = self.accum_dma(("y", b, ng), self.y_add[row0:row0 + 128, :],
                              wk_y, first=True)
        if not hasattr(self, "_ych_user"):
            self._ych_user = {}
        dev = self.dbg(f"ych_{b}_{ng}", wk_y, wait=y_evs[-1])
        self._ych_user[ci % 2] = [ydma] + ([dev] if dev else [])

    def gates(self, b, layer):
        E = self.E
        A1 = self.A1.ap()
        st = self.stat.ap()
        if layer == 0:
            wev = E.dma("gpsimd", self.wr_bf.ap(),
                        self.inp["wr"].rearrange("(k p) e -> p k e", p=128))
        else:
            wev = E.dma("gpsimd", self.wr_bf.ap(),
                        self.inp["wgT"].rearrange("(k p) e -> p k e", p=128))
        lg = st[:, 8:16]
        m1 = st[:, 16:17]
        m2 = st[:, 17:18]
        dd = st[:, 18:19]
        w1 = st[:, 19:20]
        w2 = st[:, 20:21]
        msk1 = st[:, 24:32]
        l2 = st[:, 32:40]
        t2 = st[:, 40:48]
        for tc in range(16):
            bank = 6 + tc % 2
            ps = self.PS[bank].ap()
            self.ps_w(bank)
            E.w("tensor", wev, self.xnT_ev[-1], self.consts_ev[-1])
            for k in range(8):
                tok = E.op("tensor", "matmul", ps[:, 0:8],
                           A1[:, 2048 * k + 128 * tc:2048 * k + 128 * tc + 128],
                           self.wr_bf.ap()[:, 8 * k:8 * k + 8],
                           start=(k == 0), stop=False)
            if layer == 0:
                tok = E.op("tensor", "matmul", ps[:, 0:8],
                           self.c_ones_rowb.ap()[0:1, :],
                           self.brrow.ap()[0:1, 0:8],
                           start=False, stop=True)
            else:
                tok.k["stop"] = True
            ev = E.inc("tensor", tok, "tv", 1)
            self.ps_note_w(bank, ev)
            self.ps_r("vector", bank)
            lgev = E.sig("vector", "tensor_copy", lg, ps[:, 0:8])
            self.ps_note_r(bank, lgev)
            E.sig("vector", "reduce_max", m1, lg, axis=AX.X)
            E.sig("vector", "tensor_scalar", out=msk1, in0=lg, scalar1=m1,
                  scalar2=None, op0=ALU.is_ge)
            E.sig("vector", "scalar_tensor_tensor", out=l2, in0=msk1,
                  scalar=NEG, in1=lg, op0=ALU.mult, op1=ALU.add)
            E.sig("vector", "reduce_max", m2, l2, axis=AX.X)
            dev = E.sig("vector", "tensor_sub", dd, m2, m1)
            E.w("scalar", dev)
            eev = E.sig("scalar", "activation", out=dd, in_=dd, func=AF.Exp)
            E.w("vector", eev)
            E.sig("vector", "tensor_scalar_add", dd, dd, 1.0)
            E.sig("vector", "reciprocal", w1, dd)
            E.sig("vector", "tensor_scalar", out=w2, in0=w1, scalar1=-1.0,
                  scalar2=1.0, op0=ALU.mult, op1=ALU.add)
            E.sig("vector", "tensor_scalar", out=t2, in0=l2, scalar1=m2,
                  scalar2=None, op0=ALU.is_ge)
            E.sig("vector", "tensor_scalar_mul", t2, t2, w2)
            gev = E.sig("vector", "scalar_tensor_tensor",
                        out=self.g_all.ap()[:, 8 * tc:8 * tc + 8],
                        in0=msk1, scalar=w1, in1=t2,
                        op0=ALU.mult, op1=ALU.add)
            self.g_ev = gev
        self.dbg(f"gates{layer}_{b}", self.g_all.ap(), wait=self.g_ev)

    def phase_C(self, b):
        E = self.E
        self.join()
        A1 = self.A1.ap()
        self.gates(b, 0)
        w1v = self.inp["w1"].rearrange("e (k p) h -> e p k h", p=128)
        w2v = self.inp["w2"].rearrange("e (k p) d -> e p k d", p=128)

        def load_unit(u):
            e, qt = divmod(u, 4)
            WB = (self.A2 if u % 2 == 0 else self.A3).ap()
            wkey = "C_A2" if u % 2 == 0 else "C_A3"
            prev = self._wbuf_user.get(wkey)
            if prev is not None:
                E.w("gpsimd", prev)
            E.dma("gpsimd", WB[:, 0:8192].rearrange("p (k d) -> p k d", d=1024),
                  w1v[e, :, :, 1024 * qt:1024 * qt + 1024])
            wev = E.dma("gpsimd", WB[:, 8192:16384].rearrange("p (k d) -> p k d", d=1024),
                        w2v[e][:, 8 * qt:8 * qt + 8, :])
            self._cld[u] = wev

        self._cld = {}
        b2evs = {}
        load_unit(0)
        for u in range(32):
            e, qt = divmod(u, 4)
            WB = (self.A2 if u % 2 == 0 else self.A3).ap()
            wkey = "C_A2" if u % 2 == 0 else "C_A3"
            wev = self._cld[u]
            prev = self._wbuf_user.get("b1c")
            if prev is not None:
                E.w("sync", prev)
            bev = E.dma("sync", self.b1c_buf.ap(),
                        self.inp["b1c"][(e * 4 + qt) * 128:(e * 4 + qt) * 128 + 128, :])
            if qt == 0:
                prev = self._wbuf_user.get("b2r")
                if prev is not None:
                    E.w("gpsimd", prev)
                b2evs[e] = E.dma("gpsimd", self.b2row.ap(), self.inp["b2r"][e:e + 1, :])
            b2ev = b2evs.get(e)
            if u + 1 < 32:
                load_unit(u + 1)
            A4 = self.A4.ap()
            last_h = None
            last_y = None
            for tp in range(2):
                for m in range(8):
                    b0 = 2 * (m % 3)
                    for k in range(8):
                        for half in range(2):
                            bank = b0 + half
                            ps = self.PS[bank].ap()
                            if k == 0:
                                self.ps_w(bank)
                                E.w("tensor", wev, self.xnT_ev[-1])
                            tok = E.op("tensor", "matmul", ps[:, 0:512],
                                       WB[:, 1024 * k + 128 * m:1024 * k + 128 * m + 128],
                                       A1[:, 2048 * k + 1024 * tp + 512 * half:
                                          2048 * k + 1024 * tp + 512 * half + 512],
                                       start=(k == 0), stop=(k == 7))
                            if k == 7:
                                ev = E.inc("tensor", tok, "tv", 1)
                                self.ps_note_w(bank, ev)
                                last_h = ev
                    for half in range(2):
                        bank = b0 + half
                        ps = self.PS[bank].ap()
                        self.ps_r("scalar", bank)
                        E.w("scalar", bev)
                        hev = E.sig("scalar", "activation",
                                    out=A4[:, 1024 * m + 512 * half:1024 * m + 512 * half + 512],
                                    in_=ps[:, 0:512], func=AF.Gelu,
                                    bias=self.b1c_buf.ap()[:, m:m + 1])
                        self.ps_note_r(bank, hev)
                        self._hT_ev = hev
                for yh in range(2):
                    for bank in range(8):
                        self.ps_w(bank)
                    E.w("tensor", self._hT_ev)
                    yevs = {}
                    for kk in range(8):
                        for sub in range(4):
                            for fh in range(2):
                                bank = 2 * sub + fh
                                ps = self.PS[bank].ap()
                                tok = E.op("tensor", "matmul", ps[:, 0:512],
                                           A4[:, 1024 * kk + 512 * yh + 128 * sub:
                                              1024 * kk + 512 * yh + 128 * sub + 128],
                                           WB[:, 8192 + 1024 * kk + 512 * fh:
                                              8192 + 1024 * kk + 512 * fh + 512],
                                           start=(kk == 0),
                                           stop=(kk == 7 and qt != 0))
                                if kk == 7 and qt != 0:
                                    yevs[bank] = E.inc("tensor", tok, "tv", 1)
                                    self.ps_note_w(bank, yevs[bank])
                    tmpb = self.WK.ap()[:, 0:4096]
                    prevs = self._ctmp_user.get(0)
                    tev = None
                    for sub in range(4):
                        tc = 8 * tp + 4 * yh + sub
                        for fh in range(2):
                            bank = 2 * sub + fh
                            ps = self.PS[bank].ap()
                            if qt == 0:
                                E.w("tensor", b2ev)
                                tok = E.op("tensor", "matmul", ps[:, 0:512],
                                           self.c_ones_rowb.ap()[0:1, :],
                                           self.b2row.ap()[0:1, 512 * fh:512 * fh + 512],
                                           start=False, stop=True)
                                ev = E.inc("tensor", tok, "tv", 1)
                                self.ps_note_w(bank, ev)
                            else:
                                ev = yevs[bank]
                            last_y = ev
                            tmp = tmpb[:, 512 * (2 * sub + fh):512 * (2 * sub + fh) + 512]
                            if prevs is not None:
                                E.w("vector", prevs)
                                prevs = None
                            self.ps_r("vector", bank)
                            E.w("vector", self.g_ev)
                            tev = E.sig("vector", "tensor_scalar_mul", tmp,
                                        ps[:, 0:512],
                                        self.g_all.ap()[:, 8 * tc + e:8 * tc + e + 1])
                            self.ps_note_r(bank, tev)
                    E.w("gpsimd", tev)
                    row0 = b * T + 1024 * tp + 512 * yh
                    dev = self.accum_dma(
                        ("y", b, 2 * tp + yh),
                        self.y_add[row0:row0 + 512, :].rearrange(
                            "(s p) (f c) -> p s f c", p=128, c=512),
                        tmpb.rearrange("p (s f c) -> p s f c", f=2, c=512))
                    self._ctmp_user[0] = dev
            self._wbuf_user[wkey] = last_y
            self._wbuf_user["b1c"] = last_h
            if qt == 3:
                self._wbuf_user["b2r"] = last_y

    def phase_D(self, b):
        E = self.E
        self.join()
        self.ln_groups(b, self.inp["x"], self.y_add, self.x1_d, self.A1)
        self.dbg(f"xn1T_{b}", self.A1.ap(), wait=self.xnT_ev[-1])

    def phase_E(self, b):
        E = self.E
        self.join()
        A1, A2, A3 = self.A1.ap(), self.A2.ap(), self.A3.ap()
        A4, A5, A6 = self.A4.ap(), self.A5.ap(), self.A6.ap()
        PS = [p.ap() for p in self.PS]
        xnT_ok = self.xnT_ev[-1]
        E.w("gpsimd", self._wbuf_user.get("E_A3"))
        E.dma("gpsimd", A3[:, 0:8192].rearrange("p (k d) -> p k d", d=1024),
              self.inp["l1wq"].rearrange("(k p) d -> p k d", p=128))
        E.dma("gpsimd", A3[:, 8192:10240].rearrange("p (k d) -> p k d", d=256),
              self.inp["l1wk"].rearrange("(k p) d -> p k d", p=128))
        wev = E.dma("gpsimd", A3[:, 10240:12288].rearrange("p (k d) -> p k d", d=256),
                    self.inp["l1wv"].rearrange("(k p) d -> p k d", p=128))
        lastq = None
        for m in range(8):
            for n in range(4):
                bank = (m * 4 + n) % 2
                ps = self.PS[bank].ap()
                self.ps_w(bank)
                E.w("tensor", wev, xnT_ok)
                for k in range(8):
                    tok = E.op("tensor", "matmul", ps[:, 0:512],
                               A3[:, 1024 * k + 128 * m:1024 * k + 128 * m + 128],
                               A1[:, 2048 * k + 512 * n:2048 * k + 512 * n + 512],
                               start=(k == 0), stop=(k == 7))
                ev = E.inc("tensor", tok, "tv", 1)
                self.ps_note_w(bank, ev)
                lastq = ev
                self.ps_r("scalar", bank)
                qev = E.sig("scalar", "activation",
                            out=A2[:, 2048 * m + 512 * n:2048 * m + 512 * n + 512],
                            in_=ps[:, 0:512], func=AF.Copy, scale=0.125)
                self.ps_note_r(bank, qev)
        kpre = self.WK.ap()[:, 0:4096]
        lastk = None
        for m in range(2):
            for n in range(4):
                bank = 2 + (m * 4 + n) % 2
                ps = self.PS[bank].ap()
                self.ps_w(bank)
                for k in range(8):
                    tok = E.op("tensor", "matmul", ps[:, 0:512],
                               A3[:, 8192 + 256 * k + 128 * m:8192 + 256 * k + 128 * m + 128],
                               A1[:, 2048 * k + 512 * n:2048 * k + 512 * n + 512],
                               start=(k == 0), stop=(k == 7))
                ev = E.inc("tensor", tok, "tv", 1)
                self.ps_note_w(bank, ev)
                lastk = ev
                self.ps_r("vector", bank)
                kev = E.sig("vector", "tensor_copy",
                            kpre[:, 2048 * m + 512 * n:2048 * m + 512 * n + 512],
                            ps[:, 0:512])
                self.ps_note_r(bank, kev)
        lastv = None
        for tc in range(16):
            bank = 4 + tc % 2
            ps = self.PS[bank].ap()
            self.ps_w(bank)
            for k in range(8):
                tok = E.op("tensor", "matmul", ps[:, 0:256],
                           A1[:, 2048 * k + 128 * tc:2048 * k + 128 * tc + 128],
                           A3[:, 10240 + 256 * k:10240 + 256 * k + 256],
                           start=(k == 0), stop=(k == 7))
            ev = E.inc("tensor", tok, "tv", 1)
            self.ps_note_w(bank, ev)
            lastv = ev
            self.ps_r("scalar", bank)
            for g4 in range(4):
                tok = E.op("scalar", "activation",
                           out=A5[:, 1040 * g4 + 65 * tc:1040 * g4 + 65 * tc + 64],
                           in_=ps[:, 64 * g4:64 * g4 + 64], func=AF.Copy)
            vev = E.inc("scalar", tok, "sv", 1)
            self.ps_note_r(bank, vev)
            self._vtok_ev = vev
        for g4 in range(4):
            va = A5[:, 1040 * g4:1040 * g4 + 1040]
            ev = E.sig("vector", "memset",
                       va.rearrange("p (c w) -> p c w", w=65)[:, :, 64:65], 1.0)
        self._vones_ev = ev
        sqb = A4[:, 0:2048]
        nq_tok = self.wk(4)[:, 0:256]
        k2_tok = self.wk(4)[:, 256:320]
        E.w("scalar", qev, kev)
        for tcg in range(8):
            prev = getattr(self, "_sqb_user", None)
            if prev is not None:
                E.w("scalar", prev)
            for k in range(8):
                tok = E.op("scalar", "activation", out=sqb[:, 256 * k:256 * k + 256],
                           in_=A2[:, 2048 * k + 256 * tcg:2048 * k + 256 * tcg + 256],
                           func=AF.Square)
            sev = E.inc("scalar", tok, "sv", 1)
            for half in range(2):
                tc = 2 * tcg + half
                bank = 6 + half
                ps = self.PS[bank].ap()
                self.ps_w(bank)
                E.w("tensor", sev)
                for k in range(8):
                    tok = E.op("tensor", "matmul", ps[:, 2 * k:2 * k + 2],
                               sqb[:, 256 * k + 128 * half:256 * k + 128 * half + 128],
                               self.c_hmap2.ap()[:],
                               start=(k == 0), stop=(k == 7))
                ev = E.inc("tensor", tok, "tv", 1)
                self.ps_note_w(bank, ev)
                self._sqb_user = ev
                self.ps_r("vector", bank)
                nev = E.sig("vector", "tensor_copy",
                            nq_tok[:, 16 * tc:16 * tc + 16], ps[:, 0:16])
                self.ps_note_r(bank, nev)
        for tc in range(16):
            bank = 6 + tc % 2
            ps = self.PS[bank].ap()
            prev = getattr(self, "_sqb_user", None)
            if prev is not None:
                E.w("scalar", prev)
            for c in range(2):
                tok = E.op("scalar", "activation", out=sqb[:, 256 * c:256 * c + 128],
                           in_=kpre[:, 2048 * c + 128 * tc:2048 * c + 128 * tc + 128],
                           func=AF.Square)
            sev = E.inc("scalar", tok, "sv", 1)
            self._sqk_ev = None
            self.ps_w(bank)
            E.w("tensor", sev)
            for c in range(2):
                tok = E.op("tensor", "matmul", ps[:, 32 + 2 * c:32 + 2 * c + 2],
                           sqb[:, 256 * c:256 * c + 128],
                           self.c_hmap2.ap()[:],
                           start=(c == 0), stop=(c == 1))
            ev = E.inc("tensor", tok, "tv", 1)
            self.ps_note_w(bank, ev)
            self._sqb_user = ev
            self.ps_r("vector", bank)
            nev = E.sig("vector", "tensor_copy",
                        k2_tok[:, 4 * tc:4 * tc + 4], ps[:, 32:36])
            self.ps_note_r(bank, nev)
        for tc in range(16):
            bank = 6 + tc % 2
            ps = self.PS[bank].ap()
            self.ps_w(bank)
            E.w("tensor", nev)
            tok = E.op("tensor", "transpose", ps[0:16, 64:192],
                       nq_tok[:, 16 * tc:16 * tc + 16], self.c_ident.ap()[:])
            tok = E.op("tensor", "transpose", ps[0:4, 192:320],
                       k2_tok[:, 4 * tc:4 * tc + 4], self.c_ident.ap()[:])
            ev = E.inc("tensor", tok, "tv", 1)
            self.ps_note_w(bank, ev)
            self.ps_r("vector", bank)
            E.sig("vector", "tensor_copy", self.nbq.ap()[:, 128 * tc:128 * tc + 128],
                  ps[0:16, 64:192])
            cev = E.sig("vector", "tensor_copy", self.nbk.ap()[:, 128 * tc:128 * tc + 128],
                        ps[0:4, 192:320])
            self.ps_note_r(bank, cev)
            self._nb_tr_ev = ev
        kmx = self.statB.ap()[0:4, 0:1]
        kmxev = E.sig("vector", "reduce_max", kmx, self.nbk.ap()[:], axis=AX.X)
        self.ps_w(6)
        E.w("tensor", kmxev)
        tok = E.op("tensor", "matmul", self.PS[6].ap()[0:16, 0:1],
                   self.c_rep4.ap()[:], kmx, start=True, stop=True)
        ev = E.inc("tensor", tok, "tv", 1)
        self.ps_note_w(6, ev)
        self.ps_r("vector", 6)
        k2ev = E.sig("vector", "tensor_copy", self.k2m.ap()[:],
                     self.PS[6].ap()[0:16, 0:1])
        self.ps_note_r(6, k2ev)
        mulv = E.sig("vector", "tensor_scalar_mul", self.nbq.ap()[:],
                     self.nbq.ap()[:], self.k2m.ap()[:])
        E.w("scalar", mulv)
        bev = E.sig("scalar", "activation", out=self.nbq.ap()[:],
                    in_=self.nbq.ap()[:], func=AF.Sqrt, scale=4096.0)
        E.w("vector", bev)
        nbv = E.sig("vector", "tensor_scalar_mul", self.nbq.ap()[:],
                    self.nbq.ap()[:], -1.0)
        E.w("gpsimd", nbv)
        self._negb_ev = E.dma("gpsimd", self.scr_nb, self.nbq.ap()[:])
        qsh = A6[:, 0:2048]
        t1b = A6[:, 2048:4096]
        rope_done_q = []
        for c in range(8):
            ch = A2[:, 2048 * c:2048 * c + 2048]
            prevs = [x for x in (getattr(self, "_qsh_user", None),)]
            E.w("sync", qev, *(p for p in prevs if p))
            sevs = []
            for blk, src_blk, in ((0, 1), (1, 0), (2, 3), (3, 2)):
                sevs.append(E.dma("sync", qsh[32 * blk:32 * blk + 32, :],
                                  ch[32 * src_blk:32 * src_blk + 32, :]))
            E.w("vector", *sevs)
            E.sig("vector", "tensor_tensor", t1b, qsh, self.c_sin.ap(), ALU.mult)
            E.sig("vector", "tensor_tensor", ch, ch, self.c_cos.ap(), ALU.mult)
            rev = E.sig("vector", "tensor_add", ch, ch, t1b)
            rope_done_q.append(rev)
            self._qsh_user = rev
        ksh = self.wk(4)[:, 0:1024]
        rope_done_k = []
        for c in range(2):
            ch = kpre[:, 2048 * c:2048 * c + 2048]
            for half in range(2):
                chh = ch[:, 1024 * half:1024 * half + 1024]
                E.w("sync", kev, self._nb_tr_ev, *rope_done_k[-1:])
                sevs = []
                for blk, src_blk in ((0, 1), (1, 0), (2, 3), (3, 2)):
                    sevs.append(E.dma("sync", ksh[32 * blk:32 * blk + 32, :],
                                      chh[32 * src_blk:32 * src_blk + 32, :]))
                E.w("vector", *sevs)
                cs = self.c_sin.ap()[:, 1024 * half:1024 * half + 1024]
                cc = self.c_cos.ap()[:, 1024 * half:1024 * half + 1024]
                t1f = self.wk(5)[:, 0:1024]
                E.sig("vector", "tensor_tensor", t1f, ksh, cs, ALU.mult)
                E.sig("vector", "tensor_tensor", chh, chh, cc, ALU.mult)
                rev = E.sig("vector", "tensor_add", chh, chh, t1f)
                rope_done_k.append(rev)
        self.dbg(f"vaug_{b}", A5[:, 0:4160], wait=self._vones_ev)
        self.dbg(f"qrope_{b}", A2[:, 0:16384], wait=self._qsh_user)
        self.dbg(f"krope_{b}", kpre, wait=rope_done_k[-1])
        self.dbg(f"vtok_{b}", A5[:, 0:4096], wait=self._vtok_ev)
        kaug_ev = []
        E.w("vector", *rope_done_k)
        E.w("vector", self._sqb_user)
        for g in range(4):
            c, rr = divmod(g, 2)
            dst = A4[:, 2048 * g:2048 * g + 2048]
            E.sig("vector", "tensor_copy", dst[0:64, :],
                  kpre[64 * rr:64 * rr + 64, 2048 * c:2048 * c + 2048])
            ev = E.sig("vector", "memset", dst[64:65, :], 1.0)
            kaug_ev.append(ev)
        st = self.statB.ap()
        for g in range(4):
            for hp in range(2):
                self.e_head_pair(b, g, 4 * g + 2 * hp, kaug_ev[g], rope_done_q)
        self.dbg(f"yT_{b}", A3[:, 0:16384], wait=self._yT_last)
        E.w("gpsimd", self._yT_last)
        woev = E.dma("gpsimd", A2[:, 0:8192].rearrange("p (k d) -> p k d", d=1024),
                     self.inp["l1wo"].rearrange("(k p) d -> p k d", p=128))
        self._wbuf_user["E_A3"] = self._yT_last
        E.w("vector", self._yT_last)
        tmpb = self.WK.ap()[:, 0:4096]
        prevs = self._ctmp_user.get(0)
        for tcg in range(4):
            tev = None
            for sub in range(4):
                tc = 4 * tcg + sub
                for fh in range(2):
                    bank = 6 + (2 * tc + fh) % 2
                    ps = self.PS[bank].ap()
                    self.ps_w(bank)
                    E.w("tensor", woev, self._yT_last)
                    for k in range(8):
                        tok = E.op("tensor", "matmul", ps[:, 0:512],
                                   A3[:, 2048 * k + 128 * tc:2048 * k + 128 * tc + 128],
                                   A2[:, 1024 * k + 512 * fh:1024 * k + 512 * fh + 512],
                                   start=(k == 0), stop=(k == 7))
                    ev = E.inc("tensor", tok, "tv", 1)
                    self.ps_note_w(bank, ev)
                    tmp = tmpb[:, 512 * (2 * sub + fh):512 * (2 * sub + fh) + 512]
                    if prevs is not None:
                        E.w("vector", prevs)
                        prevs = None
                    self.ps_r("vector", bank)
                    tev = E.sig("vector", "tensor_copy", tmp, ps[:, 0:512])
                    self.ps_note_r(bank, tev)
            E.w("gpsimd", tev)
            row0 = b * T + 512 * tcg
            dev = self.accum_dma(
                ("y2", b, tcg),
                self.y2_add[row0:row0 + 512, :].rearrange(
                    "(s p) (f c) -> p s f c", p=128, c=512),
                tmpb.rearrange("p (s f c) -> p s f c", f=2, c=512), first=True)
            prevs = dev
            self._ctmp_user[0] = dev

    def e_head_pair(self, b, g, h0, kaug_ev, rope_done_q):
        E = self.E
        A2, A3 = self.A2.ap(), self.A3.ap()
        A4, A5, A6 = self.A4.ap(), self.A5.ap(), self.A6.ap()
        PS = [p.ap() for p in self.PS]
        kaug = A4[:, 2048 * g:2048 * g + 2048]
        heads = (h0, h0 + 1)
        qaug = {}
        brow = {}
        qb = {}
        for i, h in enumerate(heads):
            qa = A6[:, 2048 * i:2048 * i + 2048]
            qaug[h] = qa
            prev = self._qaug_user.get(i)
            if prev is not None:
                E.w("vector", prev)
            E.w("vector", rope_done_q[h // 2], kaug_ev)
            qb[h] = E.sig("vector", "tensor_copy", qa[0:64, :],
                          A2[64 * (h % 2):64 * (h % 2) + 64,
                             2048 * (h // 2):2048 * (h // 2) + 2048])
            if prev is not None:
                E.w("gpsimd", prev)
            E.w("gpsimd", self._negb_ev, rope_done_q[-1])
            brow[h] = E.dma("gpsimd", qa[64:65, :], self.scr_nb[h:h + 1, :])
        PTB = {heads[0]: (2, 3), heads[1]: (6, 7)}
        AVB = {heads[0]: 4, heads[1]: 5}
        BCB = {heads[0]: 0, heads[1]: 1}
        for qg in range(4):
            njt = 4 * qg + 4
            for h in heads:
                self.ps_w(AVB[h])
            pend = {}
            for jt in range(njt + 1):
                if jt < njt:
                    for i, h in enumerate(heads):
                        bank = PTB[h][jt % 2]
                        ps = PS[bank]
                        self.ps_w(bank)
                        E.w("tensor", brow[h], qb[h], self._vones_ev, self._vtok_ev)
                        tok = E.op("tensor", "matmul", ps[:, 0:512],
                                   kaug[0:65, 128 * jt:128 * jt + 128],
                                   qaug[h][0:65, 512 * qg:512 * qg + 512],
                                   start=True, stop=True)
                        ev = E.inc("tensor", tok, "tv", 1)
                        self.ps_note_w(bank, ev)
                        if jt >= 4 * qg:
                            self.ps_r("vector", bank)
                            mev = E.sig("vector", "tensor_tensor", ps[:, 0:512],
                                        ps[:, 0:512],
                                        self.c_mask_j.ap()[:, 512 * (jt - 4 * qg):
                                                           512 * (jt - 4 * qg) + 512],
                                        ALU.add)
                            self._ps_wv[bank] = mev
                        slot = 2 * i + jt % 2
                        ptb = self.ptbuf.ap()[:, 512 * slot:512 * slot + 512]
                        prev2 = self._pt_user.get(slot)
                        if prev2 is not None:
                            E.w("scalar", prev2)
                        self.ps_r("scalar", bank)
                        pev = E.sig("scalar", "activation", out=ptb,
                                    in_=ps[:, 0:512], func=AF.Exp)
                        self.ps_note_r(bank, pev)
                        pend[(i, jt)] = (pev, ptb)
                if jt >= 1:
                    jp = jt - 1
                    for i, h in enumerate(heads):
                        pev, ptb = pend.pop((i, jp))
                        E.w("tensor", pev)
                        tok = E.op("tensor", "matmul", PS[AVB[h]][0:65, 0:512],
                                   A5[:, 1040 * g + 65 * jp:1040 * g + 65 * jp + 65],
                                   ptb, start=(jp == 0), stop=(jp == njt - 1))
                        avev = E.inc("tensor", tok, "tv", 1)
                        self.ps_note_w(AVB[h], avev)
                        self._pt_user[2 * i + jp % 2] = avev
            for i, h in enumerate(heads):
                av = AVB[h]
                rs = self.rs_row.ap()[0:1, 512 * i:512 * i + 512]
                self.ps_r("vector", av)
                prev2 = self._rs_user2.get(i)
                if prev2 is not None:
                    E.w("vector", prev2)
                rev = E.sig("vector", "reciprocal", rs, PS[av][64:65, 0:512])
                ysb = self.WK.ap()[64 * i:64 * i + 64, 5 * 1024:5 * 1024 + 512]
                yev0 = E.sig("vector", "tensor_copy", ysb, PS[av][0:64, 0:512])
                self.ps_note_r(av, yev0)
                bc = BCB[h]
                self.ps_w(bc)
                E.w("tensor", rev)
                tok = E.op("tensor", "matmul", PS[bc][0:64, 0:512],
                           self.c_ones_rowf.ap()[0:1, 0:64], rs,
                           start=True, stop=True)
                bcev = E.inc("tensor", tok, "tv", 1)
                self.ps_note_w(bc, bcev)
                self._rs_user2[i] = bcev
                self.ps_r("vector", bc)
                E.w("vector", bcev)
                yv = E.sig("vector", "tensor_tensor",
                           A3[64 * (h % 2):64 * (h % 2) + 64,
                              2048 * (h // 2) + 512 * qg:2048 * (h // 2) + 512 * qg + 512],
                           ysb, PS[bc][0:64, 0:512], ALU.mult)
                self.ps_note_r(bc, yv)
                self._yT_last = yv
        self._qaug_user[0] = ("tv", E.cnt["tv"])
        self._qaug_user[1] = ("tv", E.cnt["tv"])

    def phase_F(self, b):
        E = self.E
        self.join()
        A1, A2, A5 = self.A1.ap(), self.A2.ap(), self.A5.ap()
        self.gates(b, 1)
        fcv = self.inp["wfcT"].rearrange("e (k p) i -> e p k i", p=128)
        pjv = self.inp["wprojT"].rearrange("e (k p) d -> e p k d", p=128)

        def load_fe(e):
            par = e % 2
            fc0 = 4096 * par
            pj0 = 8192 + 4096 * par
            prev = self._wbuf_user.get(f"F_{par}")
            if prev is not None:
                E.w("gpsimd", prev)
            E.dma("gpsimd", A2[:, fc0:fc0 + 4096].rearrange("p (k d) -> p k d", d=512),
                  fcv[e])
            self._fld[e] = E.dma("gpsimd",
                                 A2[:, pj0:pj0 + 4096].rearrange("p (k d) -> p k d", d=1024),
                                 pjv[e])

        self._fld = {}
        self._fh_user = {}
        load_fe(0)
        for e in range(E1):
            par = e % 2
            fc0 = 4096 * par
            pj0 = 8192 + 4096 * par
            wkey = f"F_{par}"
            wev = self._fld[e]
            if e + 1 < E1:
                load_fe(e + 1)
            last_y = None
            for tt in range(4):
                h0 = 2048 * (tt % 2)
                for m in range(4):
                    bank = m % 2
                    ps = self.PS[bank].ap()
                    self.ps_w(bank)
                    E.w("tensor", wev, self.xnT_ev[-1])
                    for k in range(8):
                        tok = E.op("tensor", "matmul", ps[:, 0:512],
                                   A2[:, fc0 + 512 * k + 128 * m:fc0 + 512 * k + 128 * m + 128],
                                   A1[:, 2048 * k + 512 * tt:2048 * k + 512 * tt + 512],
                                   start=(k == 0), stop=(k == 7))
                    ev = E.inc("tensor", tok, "tv", 1)
                    self.ps_note_w(bank, ev)
                    self.ps_r("scalar", bank)
                    prevh = self._fh_user.get(tt % 2)
                    if prevh is not None:
                        E.w("scalar", prevh)
                    hev = E.sig("scalar", "activation",
                                out=A5[:, h0 + 512 * m:h0 + 512 * m + 512],
                                in_=ps[:, 0:512], func=AF.Gelu)
                    self.ps_note_r(bank, hev)
                tmpb = self.WK.ap()[:, 0:4096]
                prevs = self._ctmp_user.get(0)
                tev = None
                for ts4 in range(4):
                    tc = 4 * tt + ts4
                    for fh in range(2):
                        bank = 2 + (2 * ts4 + fh) % 4
                        ps = self.PS[bank].ap()
                        self.ps_w(bank)
                        E.w("tensor", hev)
                        for kk in range(4):
                            tok = E.op("tensor", "matmul", ps[:, 0:512],
                                       A5[:, h0 + 512 * kk + 128 * ts4:h0 + 512 * kk + 128 * ts4 + 128],
                                       A2[:, pj0 + 1024 * kk + 512 * fh:
                                          pj0 + 1024 * kk + 512 * fh + 512],
                                       start=(kk == 0), stop=(kk == 3))
                        ev = E.inc("tensor", tok, "tv", 1)
                        self.ps_note_w(bank, ev)
                        last_y = ev
                        self._fh_user[tt % 2] = ev
                        tmp = tmpb[:, 512 * (2 * ts4 + fh):512 * (2 * ts4 + fh) + 512]
                        if prevs is not None:
                            E.w("vector", prevs)
                            prevs = None
                        self.ps_r("vector", bank)
                        E.w("vector", self.g_ev)
                        tev = E.sig("vector", "tensor_scalar_mul", tmp,
                                    ps[:, 0:512],
                                    self.g_all.ap()[:, 8 * tc + e:8 * tc + e + 1])
                        self.ps_note_r(bank, tev)
                E.w("gpsimd", tev)
                row0 = b * T + 512 * tt
                dev = self.accum_dma(
                    ("y2", b, tt),
                    self.y2_add[row0:row0 + 512, :].rearrange(
                        "(s p) (f c) -> p s f c", p=128, c=512),
                    tmpb.rearrange("p (s f c) -> p s f c", f=2, c=512))
                self._ctmp_user[0] = dev
            self._wbuf_user[wkey] = last_y

    def phase_G(self, b):
        E = self.E
        self.join()
        self.dbg(f"yadd_{b}", self.y_add[b * T:(b + 1) * T, :])
        self.dbg(f"y2add_{b}", self.y2_add[b * T:(b + 1) * T, :])
        self.dbg(f"x1d_{b}", self.x1_d[b * T:(b + 1) * T, :])
        xb = self.A3.ap()[:, 0:8192].bitcast(F32)
        yb = self.A2.ap()[:, 0:8192].bitcast(F32)
        for tg in range(4):
            row0 = b * T + 512 * tg
            pv = getattr(self, "_g_user", None)
            if pv is not None:
                E.w("sync", pv)
            ev_x = E.dma("sync", xb.rearrange("p (s c) -> p s c", c=1024),
                         self.x1_d[row0:row0 + 512, :].rearrange(
                             "(s p) c -> p s c", p=128))
            ev_y = E.dma("sync", yb.rearrange("p (s c) -> p s c", c=1024),
                         self.y2_add[row0:row0 + 512, :].rearrange(
                             "(s p) c -> p s c", p=128))
            E.w("vector", ev_x, ev_y)
            aev = E.sig("vector", "tensor_add", xb, xb, yb)
            E.w("sync", aev)
            oev = E.dma("sync", self.out[row0:row0 + 512, :].rearrange(
                "(s p) c -> p s c", p=128),
                xb.rearrange("p (s c) -> p s c", c=1024))
            self._g_user = oev

    def finalize(self):
        E = self.E
        for s in ("sq", "gq"):
            if E.cnt[s]:
                E.w("sync", (s, E.cnt[s]))
        for ev in self.dbg_outs:
            E.w("sync", ev)

    def build(self):
        import contextlib
        with contextlib.ExitStack() as es:
            sems = {}
            for s in SEMS:
                sems[s] = es.enter_context(self.nc.semaphore(s))
            self.E.replay(self.nc, sems)
        return self.nc


def host_prepare(inp):
    g = lambda k: np.asarray(inp[k], dtype=np.float32)
    out = {}
    out["wq"] = np.ascontiguousarray(g("L0_attn_Wq").T)
    out["wk"] = np.ascontiguousarray(g("L0_attn_Wk").T)
    out["wv"] = np.ascontiguousarray(g("L0_attn_Wv").T)
    out["wbeta"] = np.ascontiguousarray(g("L0_attn_Wbeta").T)
    out["walpha"] = np.ascontiguousarray(g("L0_attn_Walpha").T)
    out["bq_c"] = np.ascontiguousarray(g("L0_attn_bq").reshape(8, 128).T)
    out["balpha_c"] = np.ascontiguousarray(g("L0_attn_balpha").reshape(8, 128).T)
    out["bk_r"] = g("L0_attn_bk").reshape(1, D)
    out["bv_r"] = g("L0_attn_bv").reshape(1, D)
    out["bbeta_r"] = g("L0_attn_bbeta").reshape(1, D)
    out["w1"] = g("L0_ffn_W1")
    out["w2"] = g("L0_ffn_W2")
    b1 = g("L0_ffn_b1")
    out["b1c"] = np.ascontiguousarray(
        b1.reshape(E0, 4, 8, 128).transpose(0, 1, 3, 2)).reshape(E0 * 4 * 128, 8)
    out["b2r"] = g("L0_ffn_b2")
    out["wr"] = g("L0_ffn_Wr")
    out["br_r"] = g("L0_ffn_br").reshape(1, E0)
    out["l1wq"] = g("L1_attn_Wq")
    out["l1wk"] = g("L1_attn_Wk")
    out["l1wv"] = g("L1_attn_Wv")
    out["l1wo"] = g("L1_attn_Wo")
    out["wgT"] = np.ascontiguousarray(g("L1_ffn_w_gate").T)
    out["wfcT"] = np.ascontiguousarray(g("L1_ffn_w_fc").transpose(0, 2, 1))
    out["wprojT"] = np.ascontiguousarray(g("L1_ffn_w_proj").transpose(0, 2, 1))
    out["ident"] = np.eye(128, dtype=np.float32)
    out["ones_col"] = np.ones((128, 1), np.float32)
    out["ones_row"] = np.ones((1, 128), np.float32)
    out["umat"] = np.triu(np.ones((128, 128), np.float32))
    c1 = np.zeros((128, 128), np.float32)
    c1[0, :] = 1.0
    for k in range(1, 128):
        c1[k, k] = 1.0
    out["c1mat"] = c1
    r = np.zeros((128, 128), np.float32)
    for k in range(1, 128):
        r[k, :k] = NEG
    out["rconst"] = np.ascontiguousarray(np.tile(r, (1, H)))
    hm = np.zeros((128, 2), np.float32)
    hm[0:64, 0] = 1.0 / 64
    hm[64:128, 1] = 1.0 / 64
    out["hmap2"] = hm
    half = HD // 2
    inv = 1.0 / (10000.0 ** (np.arange(half, dtype=np.float64) / half))
    tpos = np.arange(T, dtype=np.float64)
    fr = np.outer(tpos, inv)
    emb = np.concatenate([fr, fr], axis=1)
    cos = np.cos(emb).T.astype(np.float32)
    sin = np.sin(emb).T.astype(np.float32)
    sin[:half, :] *= -1.0
    out["rope_cos"] = np.ascontiguousarray(np.tile(cos, (2, 1)))
    out["rope_sin"] = np.ascontiguousarray(np.tile(sin, (2, 1)))
    mj = np.zeros((128, 4, 512), np.float32)
    for rr in range(4):
        for jj in range(128):
            mj[jj, rr, : min(512, 128 * rr + jj)] = NEG
    out["mask_j"] = np.ascontiguousarray(mj.reshape(128, 4 * 512))
    mq = np.zeros((128, 4, 512), np.float32)
    for rr in range(4):
        for qq in range(128):
            lo = 128 * rr + qq + 1
            if lo < 512:
                mq[qq, rr, lo:] = NEG
    out["mask_q"] = np.ascontiguousarray(mq.reshape(128, 4 * 512))
    rep4 = np.zeros((4, 16), np.float32)
    for gg in range(4):
        rep4[gg, 4 * gg:4 * gg + 4] = 1.0
    out["rep4"] = rep4
    return out


_BUILD_OVERRIDE = {}


def kernel(**inputs):
    shared = host_prepare(inputs)
    x = np.asarray(inputs["L0_attn_x"], dtype=np.float32)
    p = Prog(**_BUILD_OVERRIDE)
    build_all(p)
    p.finalize()
    nc = p.build()
    in_maps = []
    for c in range(NCORES):
        m = dict(shared)
        m["x"] = np.ascontiguousarray(x[B * c: B * (c + 1)].reshape(B * T, D))
        in_maps.append(m)
    res = run_bass_kernel_spmd(nc, in_maps, core_ids=list(range(NCORES)))
    out = np.concatenate(
        [r["out"].reshape(B, T, D) for r in res.results], axis=0)
    return out


def build_all(p):
    p.load_consts()
    for b in p.batches:
        if "A" in p.phases:
            p.phase_A(b)
        if "B" in p.phases:
            p.phase_B(b)
        if "C" in p.phases:
            p.phase_C(b)
        if "D" in p.phases:
            p.phase_D(b)
        if "E" in p.phases:
            p.phase_E(b)
        if "F" in p.phases:
            p.phase_F(b)
        if "G" in p.phases:
            p.phase_G(b)


# version 12 (final)
# speedup vs baseline: 1.1857x; 1.1857x over previous
import math
from collections import defaultdict

import numpy as np
import ml_dtypes

import concourse.bass as bass
import concourse.mybir as mybir
from concourse.bass_utils import run_bass_kernel_spmd

F32 = mybir.dt.float32
BF16 = mybir.dt.bfloat16
AF = mybir.ActivationFunctionType
ALU = mybir.AluOpType
AX = mybir.AxisListType

NCORES = 8
B = 4
T = 2048
D = 1024
H = 16
HD = 64
CH = 128
NCH = T // CH
KVH = 4
E0 = 8
H0 = 4096
E1 = 8
I1 = 512
NEG = -1.0e9

BF = ml_dtypes.bfloat16


class CallTok:
    __slots__ = ("meth", "a", "k", "incs")

    def __init__(self, meth, a, k):
        self.meth = meth
        self.a = a
        self.k = k
        self.incs = []

    def then_inc(self, sem_name, n):
        self.incs.append((sem_name, n))
        return self


class Rec:
    def __init__(self):
        self.calls = []

    def emit(self, meth, *a, **k):
        tok = CallTok(meth, a, k)
        self.calls.append(tok)
        return tok


ENGS = ("tensor", "vector", "scalar", "gpsimd", "sync")
SEMS = ("tv", "vv", "sv", "gq", "sq")
OWN = {"tensor": "tv", "vector": "vv", "scalar": "sv"}


class Em:

    def __init__(self):
        self.rec = {e: Rec() for e in ENGS}
        self.cnt = defaultdict(int)
        self.last_wait = {e: defaultdict(int) for e in ENGS}
        self._pend = {}

    def op(self, eng, meth, *a, **k):
        if eng in ("vector", "scalar"):
            pend = self._pend.get(eng)
            if pend is not None:
                s, v = pend
                if self.last_wait[eng][s] < v:
                    self.rec[eng].emit("wait_ge", s, v)
                    self.last_wait[eng][s] = v
                self._pend[eng] = None
        tok = self.rec[eng].emit(meth, *a, **k)
        if eng in ("vector", "scalar") and meth != "wait_ge":
            out_ap = k.get("out")
            if out_ap is None and a and hasattr(a[0], "free_size"):
                out_ap = a[0]
            small = False
            if out_ap is not None:
                try:
                    small = out_ap.free_size() <= 16
                except Exception:
                    small = False
            if small:
                ev = self.inc(eng, tok, OWN[eng], 1)
                self._pend[eng] = ev
        return tok

    def inc(self, eng, tok, sem, n=1):
        if tok.incs:
            s0 = tok.incs[0][0]
            return (s0, self.cnt[s0])
        self.cnt[sem] += n
        tok.then_inc(sem, n)
        ev = (sem, self.cnt[sem])
        if eng == "tensor" and OWN.get(eng) == sem:
            self.last_wait[eng][sem] = ev[1]
        return ev

    def sig(self, eng, meth, *a, **k):
        tok = self.op(eng, meth, *a, **k)
        if tok.incs:
            return (tok.incs[0][0], self.cnt[tok.incs[0][0]])
        return self.inc(eng, tok, OWN[eng], 1)

    def w(self, eng, *evs):
        best = {}
        for ev in evs:
            if ev is None:
                continue
            s, v = ev
            if v > best.get(s, -1):
                best[s] = v
        for s, v in best.items():
            if self.last_wait[eng][s] >= v:
                continue
            self.rec[eng].emit("wait_ge", s, v)
            self.last_wait[eng][s] = v

    def dma(self, eng, out, in_, **k):
        sem = "gq" if eng == "gpsimd" else "sq"
        tok = self.op(eng, "dma_start", out=out, in_=in_, **k)
        return self.inc(eng, tok, sem, 16)

    def replay(self, nc, sems):
        def run(eng_obj, rec):
            for tok in rec.calls:
                if tok.meth == "wait_ge":
                    eng_obj.wait_ge(sems[tok.a[0]], tok.a[1])
                    continue
                ins = getattr(eng_obj, tok.meth)(*tok.a, **tok.k)
                for s, n in tok.incs:
                    ins.then_inc(sems[s], n)

        with nc.Block() as block:
            if self.rec["tensor"].calls:
                block.tensor(lambda t: run(t, self.rec["tensor"]))
            if self.rec["vector"].calls:
                block.vector(lambda v: run(v, self.rec["vector"]))
            if self.rec["scalar"].calls:
                block.scalar(lambda s: run(s, self.rec["scalar"]))
            if self.rec["gpsimd"].calls:
                block.gpsimd(lambda g: run(g, self.rec["gpsimd"]))
            if self.rec["sync"].calls:
                block.sync(lambda s: run(s, self.rec["sync"]))


class Prog:
    def __init__(self, debug=(), batches=None, phases=("A", "B", "C", "D", "E", "F", "G"),
                 b_nchunks=99, b_stage=99, b_nheads=99, b_nproj=99, b_bstep=99):
        self.b_nchunks = b_nchunks
        self.b_nproj = b_nproj
        self.b_bstep = b_bstep
        self.b_stage = b_stage
        self.b_nheads = b_nheads
        self.debug = set(debug)
        self.batches = batches if batches is not None else list(range(B))
        self.phases = set(phases)
        self.nc = bass.Bass("TRN2", target_bir_lowering=False,
                            detect_race_conditions=False)
        self.E = Em()
        self.dbg_outs = []
        self.dbg_n = 0
        self.region_ev = {}
        self._psum_bf = {}
        self._ctmp_user = {}
        self._qaug_user = {}
        self._rs_user2 = {}
        self._pt_user = {}
        self._wbuf_user = {}
        self._xt_user = {}
        self.declare_io()
        self.alloc_sbuf()
        self.alloc_psum()

    def declare_io(self):
        nc = self.nc
        d = {}

        def inp(name, shape, dt=F32):
            d[name] = nc.dram_tensor(name, list(shape), dt, kind="ExternalInput").ap()

        inp("x", (B * T, D))
        for w in ("wq", "wk", "wv", "wbeta", "walpha"):
            inp(w, (D, D))
        inp("bq_c", (128, 8))
        inp("balpha_c", (128, 8))
        inp("bk_r", (1, D))
        inp("bv_r", (1, D))
        inp("bbeta_r", (1, D))
        inp("w1", (E0, D, H0))
        inp("w2", (E0, H0, D))
        inp("b1c", (E0 * 4 * 128, 8))
        inp("b2r", (E0, D))
        inp("wr", (D, E0))
        inp("br_r", (1, E0))
        inp("l1wq", (D, D))
        inp("l1wk", (D, KVH * HD))
        inp("l1wv", (D, KVH * HD))
        inp("l1wo", (D, D))
        inp("wgT", (D, E1))
        inp("wfcT", (E1, D, I1))
        inp("wprojT", (E1, I1, D))
        inp("ident", (128, 128))
        inp("ones_col", (128, 1))
        inp("ones_row", (1, 128))
        inp("umat", (128, 128))
        inp("c1mat", (128, 128))
        inp("rconst", (128, H * CH))
        inp("hmap2", (128, 2))
        inp("rope_cos", (128, T))
        inp("rope_sin", (128, T))
        inp("mask_j", (128, 4 * 512))
        inp("mask_q", (128, 4 * 512))
        inp("rep4", (4, 16))
        self.inp = d

        self.out = nc.dram_tensor("out", [B * T, D], F32, kind="ExternalOutput").ap()
        self.y_add = nc.dram_tensor("y_add", [B * T, D], F32, kind="Internal").ap()
        self.x1_d = nc.dram_tensor("x1_d", [B * T, D], F32, kind="Internal").ap()
        self.y2_add = nc.dram_tensor("y2_add", [B * T, D], F32, kind="Internal").ap()
        self.scr_m = nc.dram_tensor("scr_m", [16, 128], F32, kind="Internal").ap()
        self.scr_b = nc.dram_tensor("scr_b", [16, 128], F32, kind="Internal").ap()
        self.scr_nb = nc.dram_tensor("scr_nb", [16, 2048], F32, kind="Internal").ap()

    def alloc_sbuf(self):
        nc = self.nc
        A = nc.alloc_sbuf_tensor
        self.A1 = A("A1", [128, 16384], BF16)
        self.A2 = A("A2", [128, 16384], BF16)
        self.A3 = A("A3", [128, 16384], BF16)
        self.A4 = A("A4", [128, 8192], BF16)
        self.A5 = A("A5", [128, 4352], BF16)
        self.A6 = A("A6", [128, 4096], BF16)
        self.WK = A("WK", [128, 6144], F32)
        self.c_ones_rowf = A("c_ones_rowf", [1, 128], F32)
        self.c_ident = A("c_ident", [128, 128], F32)
        self.c_identb = A("c_identb", [128, 128], BF16)
        self.c_ones_col = A("c_ones_col", [128, 1], F32)
        self.c_ones_rowb = A("c_ones_rowb", [1, 128], BF16)
        self.c_umat = A("c_umat", [128, 128], F32)
        self.c_c1 = A("c_c1", [128, 128], F32)
        self.c_rall = A("c_rall", [128, H * CH], F32)
        self.c_hmap2 = A("c_hmap2", [128, 2], BF16)
        self.c_cos = A("c_cos", [128, T], BF16)
        self.c_sin = A("c_sin", [128, T], BF16)
        self.c_mask_j = A("c_mask_j", [128, 4 * 512], BF16)
        self.c_rep4 = A("c_rep4", [4, 16], F32)
        self.bq8 = A("bq8_sb", [128, 8], F32)
        self.cbias = A("cbias", [128, 2], F32)
        self.balpha_c = A("balpha_sb", [128, 8], F32)
        self.biasrows = A("biasrows", [1, 3 * D], BF16)
        self.b2row = A("b2row", [1, D], BF16)
        self.brrow = A("brrow", [1, E0], BF16)
        self.b1c_buf = A("b1c_buf", [128, 8], F32)
        self.S_all = A("S_all", [128, 512], F32)
        self.S_bf = A("S_bf", [128, 512], BF16)
        self.bm_sb = A("bm_sb", [128, 8 * H], F32)
        self.lam_sb = A("lam_sb", [128, 16], F32)
        self.m_sb = A("m_sb", [128, 16], F32)
        self.negm_sb = A("negm_sb", [128, 16], F32)
        self.curve_sb = A("curve_sb", [128, 16], F32)
        self.mT_sb = A("mT_sb", [16, 128], F32)
        self.asum_row = A("asum_row", [1, 16], F32)
        self.asumbc = A("asumbc", [128, 8], F32)
        self.g_all = A("g_all", [128, 16 * 8], F32)
        self.wr_bf = A("wr_bf", [128, 64], BF16)
        self.nbq = A("nbq", [16, 2048], BF16)
        self.nbk = A("nbk", [4, 2048], BF16)
        self.k2m = A("k2m", [16, 1], F32)
        self.rs_row = A("rs_row", [1, 1024], F32)
        self.stat = A("stat", [128, 64], F32)
        self.statB = A("statB", [128, 64], F32)
        self.b_all = A("b_all", [128, 16], F32)
        self.expT = A("expT", [128, 2 * 128], BF16)
        self.attnT = A("attnT", [128, 2 * 128], BF16)
        self.tmp64 = A("tmp64", [128, 2 * 64], F32)
        self.ptbuf = A("ptbuf", [128, 4 * 512], BF16)

    def alloc_psum(self):
        nc = self.nc
        self.PS = [nc.alloc_psum_tensor(f"PS{i}", [128, 512], F32) for i in range(8)]

    def ps_bf(self, i):
        if i not in self._psum_bf:
            self._psum_bf[i] = self.PS[i].ap().bitcast(BF16)
        return self._psum_bf[i]

    def wk(self, i, w=1024):
        return self.WK.ap()[:, 1024 * i: 1024 * i + w]

    def dbg(self, name, ap, shape=None, wait=None):
        if name not in self.debug:
            return
        shape = shape or list(ap.shape)
        t = self.nc.dram_tensor(f"dbg_{name}", list(shape), F32,
                                kind="ExternalOutput").ap()
        eng = "sync" if ap.dtype == F32 else "gpsimd"
        if wait is not None:
            self.E.w(eng, wait)
        ev = self.E.dma(eng, t, ap)
        for e in ("tensor", "vector", "scalar", "gpsimd", "sync"):
            self.E.w(e, ev)
        self.dbg_outs.append(ev)
        return ev

    def accum_dma(self, dst_region_key, dst_ap, src_ap, first=False):
        E = self.E
        prev = self.region_ev.get(dst_region_key)
        if prev is not None:
            E.w("gpsimd", prev)
        if first:
            ev = E.dma("gpsimd", dst_ap, src_ap)
        else:
            ev = E.dma("gpsimd", dst_ap, src_ap, accum_op=ALU.add)
        self.region_ev[dst_region_key] = ev
        return ev

    def load_consts(self):
        E = self.E
        i = self.inp
        evs = []
        E.sig("vector", "memset", self.cbias.ap()[:, 0:1], 1.0)
        self.cb_ev = E.sig("vector", "memset", self.cbias.ap()[:, 1:2], 1.0e-5)
        for dst, src in (
            (self.c_ident, i["ident"]), (self.c_ones_col, i["ones_col"]),
            (self.c_umat, i["umat"]), (self.c_c1, i["c1mat"]),
            (self.c_rall, i["rconst"]), (self.c_ones_rowf, i["ones_row"]),
            (self.c_rep4, i["rep4"]),
            (self.bq8, i["bq_c"]), (self.balpha_c, i["balpha_c"]),
        ):
            evs.append(E.dma("sync", dst.ap(), src))
        for dst, src in (
            (self.c_identb, i["ident"]), (self.c_ones_rowb, i["ones_row"]),
            (self.c_hmap2, i["hmap2"]), (self.c_cos, i["rope_cos"]),
            (self.c_sin, i["rope_sin"]),
            (self.c_mask_j, i["mask_j"]),
            (self.biasrows.ap()[:, 0:D], i["bk_r"]),
            (self.biasrows.ap()[:, D:2 * D], i["bv_r"]),
            (self.biasrows.ap()[:, 2 * D:3 * D], i["bbeta_r"]),
            (self.brrow, i["br_r"]),
        ):
            evs.append(E.dma("gpsimd", dst if isinstance(dst, bass.AP) else dst.ap(), src))
        ev_c = evs[-1]
        E.w("scalar", *evs)
        ev = E.sig("scalar", "activation", out=self.bq8[:], in_=self.bq8[:],
                   func=AF.Copy, scale=0.125)
        self.consts_ev = evs + [ev]


    def phase_A(self, b, src="x", dst_xnT=None):
        if not hasattr(self, "_xb_user"):
            self._xb_user = []
        self.ln_groups(b, self.inp["x"], None, None, self.A1)
        self.dbg(f"xnT_{b}", self.A1.ap(), wait=self.xnT_ev[-1])

    def ln_tile(self, tc, xt, dstA, w_scalar=None):
        E = self.E
        if not hasattr(self, "_rs_user"):
            self._rs_user = {}
            self._xc_user = {}
        sl = tc % 2
        st = self.stat.ap()
        mu = st[:, 4 + sl:5 + sl]
        rs = st[:, 6 + sl:7 + sl]
        xc = self.wk(4 if sl == 0 else 5)
        xnb = xt.bitcast(BF16)[:, 0:1024]
        E.sig("vector", "reduce_sum", mu, xt, axis=AX.X)
        E.sig("vector", "tensor_scalar_mul", mu, mu, 1.0 / D)
        prev = self._xc_user.get(sl)
        if prev is not None:
            E.w("vector", prev)
        ev_xc = E.sig("vector", "tensor_scalar_sub", xc, xt, mu)
        E.w("scalar", ev_xc, w_scalar)
        prev = self._rs_user.get(sl)
        if prev is not None:
            E.w("scalar", prev)
        ev_ss = E.sig("scalar", "activation", out=xt, in_=xc, func=AF.Square,
                      accum_out=rs)
        self._xc_user[sl] = ev_ss
        E.sig("scalar", "activation", out=rs, in_=rs, func=AF.Ln,
              scale=1.0 / D, bias=self.cbias.ap()[:, 1:2])
        ev_rs = E.sig("scalar", "activation", out=rs, in_=rs, func=AF.Exp,
                      scale=-0.5)
        E.w("vector", ev_rs)
        ev_xn = E.sig("vector", "tensor_scalar_mul", xnb, xc, rs)
        self._rs_user[sl] = ev_xn
        bank = 6 + sl
        pbf = self.ps_bf(bank)
        self.ps_w(bank)
        E.w("tensor", ev_xn)
        for p in range(8):
            tok = E.op("tensor", "transpose", pbf[:, 128 * p:128 * p + 128],
                       xnb[:, 128 * p:128 * p + 128], self.c_identb[:])
        ev_tr = E.inc("tensor", tok, "tv", 1)
        self.ps_note_w(bank, ev_tr)
        self.ps_r("vector", bank)
        for p in range(8):
            tok = E.op("vector", "tensor_copy",
                       dstA.ap()[:, 2048 * p + 128 * tc: 2048 * p + 128 * tc + 128],
                       pbf[:, 128 * p:128 * p + 128])
        cev = E.inc("vector", tok, "vv", 1)
        self.ps_note_r(bank, cev)
        self.xnT_ev.append(cev)
        return ev_tr

    def ln_groups(self, b, x_src, y_src, x1_dst, dstA):
        E = self.E
        self.xnT_ev = []
        xb = self.WK.ap()[:, 0:4096]
        yb = self.A2.ap()[:, 0:8192].bitcast(F32)
        for tg in range(4):
            row0 = b * T + 512 * tg
            E.w("sync", *self._xb_user)
            self._xb_user = []
            ev_x = E.dma("sync", xb.rearrange("p (s c) -> p s c", c=1024),
                         x_src[row0:row0 + 512, :].rearrange(
                             "(s p) c -> p s c", p=128))
            sev = None
            if y_src is not None:
                pv = getattr(self, "_yb_user", None)
                if pv is not None:
                    E.w("sync", pv)
                ev_y = E.dma("sync", yb.rearrange("p (s c) -> p s c", c=1024),
                             y_src[row0:row0 + 512, :].rearrange(
                                 "(s p) c -> p s c", p=128))
                E.w("vector", ev_x, ev_y)
                aev = E.sig("vector", "tensor_add", xb, xb, yb)
                self._yb_user = aev
                E.w("sync", aev)
                sev = E.dma("sync", x1_dst[row0:row0 + 512, :].rearrange(
                    "(s p) c -> p s c", p=128),
                    xb.rearrange("p (s c) -> p s c", c=1024))
                self._xb_user.append(sev)
            else:
                E.w("vector", ev_x)
            for sub in range(4):
                tc = 4 * tg + sub
                xt = xb[:, 1024 * sub:1024 * sub + 1024]
                tr = self.ln_tile(tc, xt, dstA, w_scalar=sev)
            self._xb_user.append(tr)

    def ps_w(self, bank):
        for ev in getattr(self, "_ps_r", {}).get(bank, []):
            self.E.w("tensor", ev)

    def ps_note_w(self, bank, ev):
        if not hasattr(self, "_ps_wv"):
            self._ps_wv = {}
            self._ps_r = {}
        self._ps_wv[bank] = ev
        self._ps_r[bank] = []

    def ps_r(self, eng, bank):
        self.E.w(eng, self._ps_wv[bank])

    def ps_note_r(self, bank, ev):
        self._ps_r.setdefault(bank, []).append(ev)

    def join(self):
        E = self.E
        snap = [(s, E.cnt[s]) for s in SEMS if E.cnt[s] > 0]
        for e in ENGS:
            E.w(e, *snap)

    def soft_join(self):
        E = self.E
        snap = [(s, E.cnt[s]) for s in SEMS if E.cnt[s] > 0]
        for e in ENGS:
            if e != "tensor":
                E.w(e, *snap)

    def phase_B(self, b):
        E = self.E
        self.join()
        ev1 = E.sig("vector", "memset", self.S_all[:], 0.0)
        self._sbf_ev = E.sig("vector", "memset", self.S_bf[:], 0.0)
        for hb in range(2):
            if self.b_stage < 1 and hb > 0:
                break
            self.b_proj(b, hb)
            if self.b_stage < 2:
                continue
            for ci in range(8):
                if 8 * hb + ci >= self.b_nchunks:
                    break
                self.b_chunk(b, hb, ci)

    def b_proj(self, b, hb):
        E = self.E
        A1, A2, A3 = self.A1.ap(), self.A2.ap(), self.A3.ap()
        A4, A5, A6 = self.A4.ap(), self.A5.ap(), self.A6.ap()
        xnT_ok = self.xnT_ev[-1]

        wnames = ("wq", "walpha", "wk", "wv", "wbeta")
        wbuf = {}
        wload = {}

        WKEY = {"wq": "A4", "walpha": "A56", "wk": "A4", "wv": "A56",
                "wbeta": "A56"}

        def load_w(i):
            wn = wnames[i]
            key = WKEY[wn]
            if key == "A4":
                views = [A4[:, 1024 * k:1024 * k + 1024] for k in range(8)]
            else:
                views = ([A5[:, 1024 * k:1024 * k + 1024] for k in range(4)]
                         + [A6[:, 1024 * k:1024 * k + 1024] for k in range(4)])
            prev = self._wbuf_user.get(key)
            if prev is not None:
                E.w("gpsimd", prev)
            src = self.inp[wn].rearrange("(k p) d -> p k d", p=128)
            if key == "A4":
                ev = E.dma("gpsimd", A4[:, 0:8192].rearrange("p (k d) -> p k d", d=1024), src)
            else:
                E.dma("gpsimd", A5[:, 0:4096].rearrange("p (k d) -> p k d", d=1024),
                      src[:, 0:4, :])
                ev = E.dma("gpsimd", A6[:, 0:4096].rearrange("p (k d) -> p k d", d=1024),
                          src[:, 4:8, :])
            wbuf[wn] = views
            wload[wn] = ev

        load_w(0)
        load_w(1)

        def lhs_w(wn, k, m):
            v = wbuf[wn][k]
            return v[:, 128 * m:128 * m + 128]

        def rhs_w(wn, k, n):
            v = wbuf[wn][k]
            return v[:, 512 * n:512 * n + 512]

        last_use = {}

        bank_rot = [0, 1, 2, 3]
        bi = 0
        for wi, (wn, feat) in enumerate((("wq", "q"), ("walpha", "a"))):
            if wi >= self.b_nproj:
                return
            E.w("tensor", wload[wn], xnT_ok)
            for m in range(8):
                for n in range(2):
                    bank = bank_rot[bi % 4]
                    bi += 1
                    ps = self.PS[bank].ap()
                    self.ps_w(bank)
                    for k in range(8):
                        tok = E.op("tensor", "matmul", ps[:, 0:512],
                                   lhs_w(wn, k, m),
                                   A1[:, 2048 * k + 1024 * hb + 512 * n:
                                      2048 * k + 1024 * hb + 512 * n + 512],
                                   start=(k == 0), stop=(k == 7))
                    ev = E.inc("tensor", tok, "tv", 1)
                    self.ps_note_w(bank, ev)
                    last_use[wn] = ev
                    if feat == "q":
                        self.ps_r("scalar", bank)
                        rev = E.sig("scalar", "activation",
                                    out=A2[:, 1024 * m + 512 * n:1024 * m + 512 * n + 512],
                                    in_=ps[:, 0:512], func=AF.Identity, scale=0.125,
                                    bias=self.bq8.ap()[:, m:m + 1])
                        self.ps_note_r(bank, rev)
                        self.qT_ev = rev
                    else:
                        wkA = self.wk(2)[:, 0:512] if n == 0 else self.wk(2)[:, 512:1024]
                        wkB = self.wk(3)[:, 0:512] if n == 0 else self.wk(3)[:, 512:1024]
                        self.ps_r("vector", bank)
                        E.sig("vector", "tensor_scalar", out=wkA, in0=ps[:, 0:512],
                              scalar1=self.balpha_c.ap()[:, m:m + 1], scalar2=10.0,
                              op0=ALU.add, op1=ALU.min)
                        rev = E.sig("vector", "tensor_scalar", out=wkA, in0=wkA,
                                    scalar1=-10.0, scalar2=None, op0=ALU.max)
                        self.ps_note_r(bank, rev)
                        E.w("scalar", rev)
                        aev0 = E.sig("scalar", "activation", out=wkB, in_=wkA,
                                     func=AF.Abs)
                        E.sig("scalar", "activation", out=wkB, in_=wkB, func=AF.Exp,
                              scale=-1.0)
                        sev = E.sig("scalar", "activation", out=wkB, in_=wkB,
                                    func=AF.Ln, bias=self.cbias.ap()[:, 0:1])
                        E.w("vector", aev0)
                        E.sig("vector", "tensor_scalar_min", wkA, wkA, 0.0)
                        E.w("vector", sev)
                        lev = E.sig("vector", "tensor_sub",
                                    A2[:, 8192 + 1024 * m + 512 * n:
                                       8192 + 1024 * m + 512 * n + 512],
                                    wkA, wkB)
                        self.laT_ev = lev
            self._wbuf_user[WKEY[wn]] = last_use[wn]
            load_w(wi + 2)

        bias_off = {"wk": 0, "wv": D, "wbeta": 2 * D}
        for wi, wn in ((2, "wk"), (3, "wv"), (4, "wbeta")):
            if wi >= self.b_nproj:
                return
            E.w("tensor", wload[wn], self.consts_ev[-1])
            if wn == "wbeta":
                E.w("vector", last_use["wk"])
            for ci in range(8):
                for n in range(2):
                    bank = bank_rot[bi % 4]
                    bi += 1
                    ps = self.PS[bank].ap()
                    self.ps_w(bank)
                    for k in range(8):
                        E.op("tensor", "matmul", ps[:, 0:512],
                             A1[:, 2048 * k + 1024 * hb + 128 * ci:
                                2048 * k + 1024 * hb + 128 * ci + 128],
                             rhs_w(wn, k, n),
                             start=(k == 0), stop=False)
                    tok = E.op("tensor", "matmul", ps[:, 0:512],
                               self.c_ones_rowb.ap()[0:1, :],
                               self.biasrows.ap()[0:1, bias_off[wn] + 512 * n:
                                                  bias_off[wn] + 512 * n + 512],
                               start=False, stop=True)
                    ev = E.inc("tensor", tok, "tv", 1)
                    self.ps_note_w(bank, ev)
                    last_use[wn] = ev
                    if wn == "wk":
                        self.ps_r("scalar", bank)
                        rev = E.sig("scalar", "activation",
                                    out=A3[:, 1024 * ci + 512 * n:1024 * ci + 512 * n + 512],
                                    in_=ps[:, 0:512], func=AF.Copy, scale=0.125)
                        self.ps_note_r(bank, rev)
                        self.k_ev = rev
                    elif wn == "wv":
                        self.ps_r("scalar", bank)
                        rev = E.sig("scalar", "activation",
                                    out=A3[:, 8192 + 1024 * ci + 512 * n:
                                           8192 + 1024 * ci + 512 * n + 512],
                                    in_=ps[:, 0:512], func=AF.Copy)
                        self.ps_note_r(bank, rev)
                        self.v_ev = rev
                    else:
                        wkA = self.wk(4)[:, 0:512] if n == 0 else self.wk(4)[:, 512:1024]
                        wkB = self.wk(5)[:, 0:512] if n == 0 else self.wk(5)[:, 512:1024]
                        if self.b_bstep < 0:
                            self.ps_note_r(bank, ev)
                            if not hasattr(self, "_beta_scr_ev"):
                                self._beta_scr_ev = {}
                            self._beta_scr_ev[n] = None
                            self.vb_ev = ev
                            continue
                        prev = getattr(self, "_beta_scr_ev", {}).get(n)
                        if prev is not None:
                            E.w("scalar", prev)
                        self.ps_r("scalar", bank)
                        abev0 = E.sig("scalar", "activation", out=wkB, in_=ps[:, 0:512],
                              func=AF.Abs)
                        self.ps_note_r(bank, abev0)
                        E.sig("scalar", "activation", out=wkB, in_=wkB, func=AF.Exp,
                              scale=-1.0)
                        sev = E.sig("scalar", "activation", out=wkB, in_=wkB,
                                    func=AF.Ln, bias=self.cbias.ap()[:, 0:1])
                        self.ps_r("vector", bank)
                        E.w("vector", abev0)
                        mev = E.sig("vector", "tensor_scalar_max", wkA,
                                    ps[:, 0:512], 0.0)
                        self.ps_note_r(bank, mev)
                        if self.b_bstep < 1:
                            self._beta_scr_ev[n] = mev
                            self.vb_ev = mev
                            continue
                        E.w("vector", sev)
                        bev = E.sig("vector", "tensor_add", wkA, wkA, wkB)
                        if not hasattr(self, "_beta_scr_ev"):
                            self._beta_scr_ev = {}
                        if self.b_bstep < 2:
                            self._beta_scr_ev[n] = bev
                            self.vb_ev = bev
                            continue
                        E.sig("vector", "reduce_sum",
                              self.bm_sb.ap()[:, 16 * ci + 8 * n:16 * ci + 8 * n + 8],
                              wkA.rearrange("p (h k) -> p h k", k=64), axis=AX.X)
                        if self.b_bstep < 3:
                            self._beta_scr_ev[n] = bev
                            self.vb_ev = bev
                            continue
                        bt = A4[:, 1024 * ci + 512 * n:1024 * ci + 512 * n + 512]
                        vev = E.sig("vector", "tensor_copy", bt, wkA)
                        self._beta_scr_ev[n] = vev
                        self.vb_ev = vev
            self._wbuf_user[WKEY[wn]] = last_use[wn]
            if wn == "wv":
                load_w(4)
        if hb == 0:
            self.dbg(f"qT_{b}", A2[:, 0:8192], wait=self.qT_ev)
            self.dbg(f"laT_{b}", A2[:, 8192:16384], wait=self.laT_ev)
            self.dbg(f"ktok_{b}", A3[:, 0:8192], wait=self.k_ev)
            self.dbg(f"vb_{b}", A3[:, 8192:16384], wait=self.vb_ev)
            self.dbg(f"bm_{b}", self.bm_sb.ap(), wait=self.vb_ev)

    def b_chunk(self, b, hb, ci):
        E = self.E
        A2, A3, A5 = self.A2.ap(), self.A3.ap(), self.A5.ap()
        ng = 8 * hb + ci
        t0 = 1024 * hb + 128 * ci
        PS = [p.ap() for p in self.PS]

        A4 = self.A4.ap()
        kch = A3[:, 1024 * ci:1024 * ci + 1024]
        vch = A3[:, 8192 + 1024 * ci:8192 + 1024 * ci + 1024]
        bch = A4[:, 1024 * ci:1024 * ci + 1024]
        kT = A5[:, 0:1024]
        kbm = A5[:, 1024:2048]
        vbch = A5[:, 2048:3072]

        E.w("tensor", self.laT_ev, self.vb_ev, self.k_ev, self.qT_ev,
            self._sbf_ev)

        self.ps_w(4)
        for k in range(8):
            tok = E.op("tensor", "matmul", PS[4][:, 2 * k:2 * k + 2],
                       A2[:, 8192 + 1024 * k + 128 * ci:8192 + 1024 * k + 128 * ci + 128],
                       self.c_hmap2.ap()[:],
                       start=(k == 0), stop=(k == 7))
        ev = E.inc("tensor", tok, "tv", 1)
        self.ps_note_w(4, ev)
        prev = getattr(self, "_lam_user", None)
        if prev is not None:
            E.w("vector", prev)
        self.ps_r("vector", 4)
        lam_ev = E.sig("vector", "tensor_copy", self.lam_sb[:], PS[4][:, 0:16])
        self.ps_note_r(4, lam_ev)
        self.ps_w(4)
        E.op("tensor", "matmul", PS[4][:, 16:32], self.c_umat[:], self.lam_sb[:],
             start=True, stop=True)
        tok = E.op("tensor", "matmul", PS[4][0:1, 32:48], self.c_ones_col[:],
                   self.lam_sb[:], start=True, stop=True)
        ev = E.inc("tensor", tok, "tv", 1)
        self.ps_note_w(4, ev)
        self._lam_user = ev
        for prev in getattr(self, "_m_user", []):
            E.w("vector", prev)
        self.ps_r("vector", 4)
        mev = E.sig("vector", "tensor_copy", self.m_sb[:], PS[4][:, 16:32])
        nev = E.sig("vector", "tensor_scalar_mul", self.negm_sb[:], self.m_sb[:], -1.0)
        E.w("scalar", mev)
        prev = getattr(self, "_curve_user", None)
        if prev is not None:
            E.w("scalar", prev)
        cev = E.sig("scalar", "activation", out=self.curve_sb[:], in_=self.m_sb[:],
                    func=AF.Exp)
        self.ps_r("scalar", 4)
        prev = getattr(self, "_asum_user", None)
        if prev is not None:
            E.w("scalar", prev)
        aev = E.sig("scalar", "activation", out=self.asum_row[:], in_=PS[4][0:1, 32:48],
                    func=AF.Exp)
        self.ps_note_r(4, aev)
        prev = getattr(self, "_mt_user", None)
        self.ps_w(4)
        E.w("tensor", mev)
        tok = E.op("tensor", "transpose", PS[4][0:16, 48:176], self.m_sb[:],
                   self.c_ident[:])
        ev = E.inc("tensor", tok, "tv", 1)
        self.ps_note_w(4, ev)
        self.ps_r("vector", 4)
        if prev is not None:
            E.w("vector", prev)
        mtev = E.sig("vector", "tensor_copy", self.mT_sb[:], PS[4][0:16, 48:176])
        self.ps_note_r(4, mtev)
        E.w("gpsimd", mtev)
        prevT1 = getattr(self, "_rall_user", None)
        ev1 = E.dma("gpsimd", self.scr_m, self.mT_sb[:])
        self._mt_user = ev1
        self._bT_user = ev1
        E.w("gpsimd", ev1)
        if prevT1 is not None:
            E.w("gpsimd", prevT1)
        rrow_ev = E.dma("gpsimd", self.c_rall.ap()[0:1, :],
                        self.scr_m.rearrange("h c -> (h c)").unsqueeze(0))
        self.ps_w(4)
        E.w("tensor", aev)
        asr = self.asum_row.ap().rearrange("p (h two) -> p two h", two=2)
        E.op("tensor", "matmul", PS[4][0:64, 176:184],
             self.c_ones_rowf.ap()[0:1, 0:64], asr[:, 0, :], start=True, stop=True)
        tok = E.op("tensor", "matmul", PS[4][64:128, 176:184],
                   self.c_ones_rowf.ap()[0:1, 0:64], asr[:, 1, :],
                   start=True, stop=True)
        ev = E.inc("tensor", tok, "tv", 1)
        self._asum_user = ev
        self.ps_note_w(4, ev)
        self.ps_r("vector", 4)
        abev = E.sig("vector", "tensor_copy", self.asumbc[:], PS[4][:, 176:184])
        self.ps_note_r(4, abev)
        self.ps_w(2)
        E.w("tensor", self.k_ev)
        prev_kt = getattr(self, "_kT_user", None)
        pbf = self.ps_bf(2)
        for p in range(8):
            tok = E.op("tensor", "transpose", pbf[:, 128 * p:128 * p + 128],
                       kch[:, 128 * p:128 * p + 128], self.c_identb[:])
        ev = E.inc("tensor", tok, "tv", 1)
        self.ps_note_w(2, ev)
        self.ps_r("vector", 2)
        if prev_kt is not None:
            E.w("vector", prev_kt)
        ktev = E.sig("vector", "tensor_copy", kT, pbf[:, 0:1024])
        self.ps_note_r(2, ktev)
        prev_kbm = getattr(self, "_kbm_user", None)
        if prev_kbm is not None:
            E.w("vector", prev_kbm)
        kbm_ev = None
        for h in range(H):
            tok = E.op("vector", "tensor_scalar",
                       out=kbm[:, 64 * h:64 * h + 64],
                       in0=kch[:, 64 * h:64 * h + 64],
                       scalar1=self.bm_sb.ap()[:, 16 * ci + h:16 * ci + h + 1],
                       scalar2=1.0 / 64, op0=ALU.mult, op1=ALU.mult)
        kbm_ev = E.inc("vector", tok, "vv", 1)
        prev_vb = getattr(self, "_vbch_user", None)
        if prev_vb is not None:
            E.w("vector", prev_vb)
        vbch_ev = E.sig("vector", "tensor_tensor", vbch, vch, bch, ALU.mult)

        if ng == 0:
            self.dbg("lam0", self.lam_sb.ap(), wait=lam_ev)
            self.dbg("m0", self.m_sb.ap(), wait=mev)
            self.dbg("curve0", self.curve_sb.ap(), wait=cev)
            self.dbg("asum0", self.asum_row.ap(), wait=aev)
            self.dbg("mT0", self.mT_sb.ap(), wait=mtev)
            self.dbg("asumbc0", self.asumbc.ap(), wait=abev)
            self.dbg("kT0", kT, wait=ktev)
            self.dbg("kbm0", kbm, wait=kbm_ev)
        if self.b_stage < 3:
            return
        if not hasattr(self, "_slot_user"):
            self._slot_user = {}
        wk_y = self.wk(ci % 2)
        prev_y_dma = getattr(self, "_ych_user", {}).get(ci % 2) or []
        y_evs = []
        mem_ev = None
        front = {}
        for h in range(min(H, self.b_nheads) + 1):
            if h < min(H, self.b_nheads):
                tb = (0, 1)[h % 2]
                sb = (2, 3)[h % 2]
                qslice = A2[64 * (h % 2):64 * (h % 2) + 64,
                            1024 * (h // 2) + 128 * ci:1024 * (h // 2) + 128 * ci + 128]
                self.ps_w(tb)
                E.w("tensor", rrow_ev)
                tok = E.op("tensor", "matmul", PS[tb][:, 0:128], self.c_c1[:],
                           self.c_rall.ap()[:, 128 * h:128 * h + 128],
                           start=True, stop=True)
                t1ev = E.inc("tensor", tok, "tv", 1)
                self.ps_note_w(tb, t1ev)
                self._rall_user = t1ev
                E.w("scalar", nev)
                self.ps_r("scalar", tb)
                prev = self._slot_user.get(("expT", h % 2))
                if prev is not None:
                    E.w("scalar", prev)
                eev = E.sig("scalar", "activation",
                            out=self.expT.ap()[:, 128 * (h % 2):128 * (h % 2) + 128],
                            in_=PS[tb][:, 0:128], func=AF.Exp,
                            bias=self.negm_sb.ap()[:, h:h + 1])
                self.ps_note_r(tb, eev)
                self.ps_w(sb)
                tok = E.op("tensor", "matmul", PS[sb][:, 0:128],
                           kT[64 * (h % 2):64 * (h % 2) + 64,
                              128 * (h // 2):128 * (h // 2) + 128],
                           qslice, start=True, stop=True)
                stev = E.inc("tensor", tok, "tv", 1)
                self.ps_note_w(sb, stev)
                E.w("vector", eev)
                self.ps_r("vector", sb)
                prev = self._slot_user.get(("attnT", h % 2))
                if prev is not None:
                    E.w("vector", prev)
                aev2 = E.sig("vector", "tensor_tensor",
                             self.attnT.ap()[:, 128 * (h % 2):128 * (h % 2) + 128],
                             PS[sb][:, 0:128],
                             self.expT.ap()[:, 128 * (h % 2):128 * (h % 2) + 128],
                             ALU.mult)
                self.ps_note_r(sb, aev2)
                self._slot_user[("expT", h % 2)] = aev2
                front[h] = (aev2, qslice)
            if h >= 1:
                hb2 = h - 1
                g = hb2 // 2
                yb = (5, 6)[hb2 % 2]
                db = (7, 4)[g % 2]
                aev2, qslice = front.pop(hb2)
                self.ps_w(yb)
                E.w("tensor", aev2, vbch_ev)
                E.op("tensor", "matmul", PS[yb][:, 0:64],
                     self.attnT.ap()[:, 128 * (hb2 % 2):128 * (hb2 % 2) + 128],
                     vbch[:, 64 * hb2:64 * hb2 + 64], start=True, stop=True)
                tok = E.op("tensor", "matmul", PS[yb][:, 64:128], qslice,
                           self.S_bf.ap()[64 * (hb2 % 2):64 * (hb2 % 2) + 64,
                                          64 * (hb2 // 2):64 * (hb2 // 2) + 64],
                           start=True, stop=True)
                ymev = E.inc("tensor", tok, "tv", 1)
                self.ps_note_w(yb, ymev)
                mem_ev = ymev
                self._slot_user[("attnT", hb2 % 2)] = ymev
                self.ps_r("vector", yb)
                E.w("vector", cev)
                if hb2 == 0:
                    E.w("vector", *prev_y_dma)
                E.sig("vector", "tensor_scalar_mul",
                      self.tmp64.ap()[:, 64 * (hb2 % 2):64 * (hb2 % 2) + 64],
                      PS[yb][:, 64:128], self.curve_sb.ap()[:, hb2:hb2 + 1])
                yev = E.sig("vector", "scalar_tensor_tensor",
                            out=wk_y[:, 64 * hb2:64 * hb2 + 64], in0=PS[yb][:, 0:64],
                            scalar=1.0,
                            in1=self.tmp64.ap()[:, 64 * (hb2 % 2):64 * (hb2 % 2) + 64],
                            op0=ALU.mult, op1=ALU.add)
                self.ps_note_r(yb, yev)
                y_evs.append(yev)
                if hb2 % 2 == 0:
                    self.ps_w(db)
                    E.w("tensor", kbm_ev, self.vb_ev)
                    E.op("tensor", "matmul", PS[db][0:64, 0:64],
                         kbm[:, 64 * hb2:64 * hb2 + 64], vch[:, 64 * hb2:64 * hb2 + 64],
                         start=True, stop=True)
                else:
                    tok = E.op("tensor", "matmul", PS[db][64:128, 0:64],
                               kbm[:, 64 * hb2:64 * hb2 + 64], vch[:, 64 * hb2:64 * hb2 + 64],
                               start=True, stop=True)
                    dev = E.inc("tensor", tok, "tv", 1)
                    self.ps_note_w(db, dev)
                    self.ps_r("vector", db)
                    E.w("vector", abev)
                    sev = E.sig("vector", "scalar_tensor_tensor",
                                out=self.S_all.ap()[:, 64 * g:64 * g + 64],
                                in0=self.S_all.ap()[:, 64 * g:64 * g + 64],
                                scalar=self.asumbc.ap()[:, g:g + 1],
                                in1=PS[db][0:128, 0:64],
                                op0=ALU.mult, op1=ALU.add)
                    self.ps_note_r(db, sev)
        self._kT_user = mem_ev
        self._kbm_user = ("tv", self.E.cnt["tv"])
        self._vbch_user = ("tv", self.E.cnt["tv"])
        self._wbuf_user["A4"] = ("tv", self.E.cnt["tv"])
        self._m_user = [mem_ev, ("sv", self.E.cnt["sv"])]
        self._curve_user = y_evs[-1]
        E.w("vector", mem_ev)
        sbf_ev = E.sig("vector", "tensor_copy", self.S_bf[:], self.S_all[:])
        self._sbf_ev = sbf_ev
        if ng <= 2:
            self.dbg(f"S{ng}", self.S_all.ap(), wait=sbf_ev)
        if ng == 1:
            self.dbg("lam1", self.lam_sb.ap(), wait=sbf_ev)
            self.dbg("curve1", self.curve_sb.ap(), wait=sbf_ev)
            self.dbg("asumbc1", self.asumbc.ap(), wait=sbf_ev)
        E.w("gpsimd", *y_evs[-1:])
        row0 = b * T + t0
        ydma = self.accum_dma(("y", b, ng), self.y_add[row0:row0 + 128, :],
                              wk_y, first=True)
        if not hasattr(self, "_ych_user"):
            self._ych_user = {}
        dev = self.dbg(f"ych_{b}_{ng}", wk_y, wait=y_evs[-1])
        self._ych_user[ci % 2] = [ydma] + ([dev] if dev else [])

    def gates(self, b, layer):
        E = self.E
        A1 = self.A1.ap()
        st = self.stat.ap()
        if layer == 0:
            wev = E.dma("gpsimd", self.wr_bf.ap(),
                        self.inp["wr"].rearrange("(k p) e -> p k e", p=128))
        else:
            wev = E.dma("gpsimd", self.wr_bf.ap(),
                        self.inp["wgT"].rearrange("(k p) e -> p k e", p=128))
        lg = st[:, 8:16]
        m1 = st[:, 16:17]
        m2 = st[:, 17:18]
        dd = st[:, 18:19]
        w1 = st[:, 19:20]
        w2 = st[:, 20:21]
        msk1 = st[:, 24:32]
        l2 = st[:, 32:40]
        t2 = st[:, 40:48]
        for tc in range(16):
            bank = 6 + tc % 2
            ps = self.PS[bank].ap()
            self.ps_w(bank)
            E.w("tensor", wev, self.xnT_ev[-1], self.consts_ev[-1])
            for k in range(8):
                tok = E.op("tensor", "matmul", ps[:, 0:8],
                           A1[:, 2048 * k + 128 * tc:2048 * k + 128 * tc + 128],
                           self.wr_bf.ap()[:, 8 * k:8 * k + 8],
                           start=(k == 0), stop=False)
            if layer == 0:
                tok = E.op("tensor", "matmul", ps[:, 0:8],
                           self.c_ones_rowb.ap()[0:1, :],
                           self.brrow.ap()[0:1, 0:8],
                           start=False, stop=True)
            else:
                tok.k["stop"] = True
            ev = E.inc("tensor", tok, "tv", 1)
            self.ps_note_w(bank, ev)
            self.ps_r("vector", bank)
            lgev = E.sig("vector", "tensor_copy", lg, ps[:, 0:8])
            self.ps_note_r(bank, lgev)
            E.sig("vector", "reduce_max", m1, lg, axis=AX.X)
            E.sig("vector", "tensor_scalar", out=msk1, in0=lg, scalar1=m1,
                  scalar2=None, op0=ALU.is_ge)
            E.sig("vector", "scalar_tensor_tensor", out=l2, in0=msk1,
                  scalar=NEG, in1=lg, op0=ALU.mult, op1=ALU.add)
            E.sig("vector", "reduce_max", m2, l2, axis=AX.X)
            dev = E.sig("vector", "tensor_sub", dd, m2, m1)
            E.w("scalar", dev)
            eev = E.sig("scalar", "activation", out=dd, in_=dd, func=AF.Exp)
            E.w("vector", eev)
            E.sig("vector", "tensor_scalar_add", dd, dd, 1.0)
            E.sig("vector", "reciprocal", w1, dd)
            E.sig("vector", "tensor_scalar", out=w2, in0=w1, scalar1=-1.0,
                  scalar2=1.0, op0=ALU.mult, op1=ALU.add)
            E.sig("vector", "tensor_scalar", out=t2, in0=l2, scalar1=m2,
                  scalar2=None, op0=ALU.is_ge)
            E.sig("vector", "tensor_scalar_mul", t2, t2, w2)
            gev = E.sig("vector", "scalar_tensor_tensor",
                        out=self.g_all.ap()[:, 8 * tc:8 * tc + 8],
                        in0=msk1, scalar=w1, in1=t2,
                        op0=ALU.mult, op1=ALU.add)
            self.g_ev = gev
        self.dbg(f"gates{layer}_{b}", self.g_all.ap(), wait=self.g_ev)

    def phase_C(self, b):
        E = self.E
        self.soft_join()
        A1 = self.A1.ap()
        self.gates(b, 0)
        w1v = self.inp["w1"].rearrange("e (k p) h -> e p k h", p=128)
        w2v = self.inp["w2"].rearrange("e (k p) d -> e p k d", p=128)

        def load_unit(u):
            e, qt = divmod(u, 4)
            WB = (self.A2 if u % 2 == 0 else self.A3).ap()
            wkey = "C_A2" if u % 2 == 0 else "C_A3"
            prev = self._wbuf_user.get(wkey)
            if prev is not None:
                E.w("gpsimd", prev)
            E.dma("gpsimd", WB[:, 0:8192].rearrange("p (k d) -> p k d", d=1024),
                  w1v[e, :, :, 1024 * qt:1024 * qt + 1024])
            wev = E.dma("gpsimd", WB[:, 8192:16384].rearrange("p (k d) -> p k d", d=1024),
                        w2v[e][:, 8 * qt:8 * qt + 8, :])
            self._cld[u] = wev

        self._cld = {}
        b2evs = {}
        load_unit(0)
        for u in range(32):
            e, qt = divmod(u, 4)
            WB = (self.A2 if u % 2 == 0 else self.A3).ap()
            wkey = "C_A2" if u % 2 == 0 else "C_A3"
            wev = self._cld[u]
            prev = self._wbuf_user.get("b1c")
            if prev is not None:
                E.w("sync", prev)
            bev = E.dma("sync", self.b1c_buf.ap(),
                        self.inp["b1c"][(e * 4 + qt) * 128:(e * 4 + qt) * 128 + 128, :])
            if qt == 0:
                prev = self._wbuf_user.get("b2r")
                if prev is not None:
                    E.w("gpsimd", prev)
                b2evs[e] = E.dma("gpsimd", self.b2row.ap(), self.inp["b2r"][e:e + 1, :])
            b2ev = b2evs.get(e)
            if u + 1 < 32:
                load_unit(u + 1)
            A4 = self.A4.ap()
            last_h = None
            last_y = None
            for tt in range(4):
                for m in range(8):
                    bank = m % 2
                    ps = self.PS[bank].ap()
                    self.ps_w(bank)
                    E.w("tensor", wev, self.xnT_ev[-1])
                    for k in range(8):
                        tok = E.op("tensor", "matmul", ps[:, 0:512],
                                   WB[:, 1024 * k + 128 * m:1024 * k + 128 * m + 128],
                                   A1[:, 2048 * k + 512 * tt:2048 * k + 512 * tt + 512],
                                   start=(k == 0), stop=(k == 7))
                    ev = E.inc("tensor", tok, "tv", 1)
                    self.ps_note_w(bank, ev)
                    last_h = ev
                    self.ps_r("scalar", bank)
                    E.w("scalar", bev)
                    hev = E.sig("scalar", "activation",
                                out=A4[:, 512 * m:512 * m + 512],
                                in_=ps[:, 0:512], func=AF.Gelu,
                                bias=self.b1c_buf.ap()[:, m:m + 1])
                    self.ps_note_r(bank, hev)
                    self._hT_ev = hev
                tmpb = self.WK.ap()[:, 0:4096]
                prevs = self._ctmp_user.get(0)
                tev = None
                for ts4 in range(4):
                    tc = 4 * tt + ts4
                    for fh in range(2):
                        bank = 2 + (2 * ts4 + fh) % 4
                        ps = self.PS[bank].ap()
                        self.ps_w(bank)
                        E.w("tensor", self._hT_ev)
                        for kk in range(8):
                            tok = E.op("tensor", "matmul", ps[:, 0:512],
                                       A4[:, 512 * kk + 128 * ts4:512 * kk + 128 * ts4 + 128],
                                       WB[:, 8192 + 1024 * kk + 512 * fh:
                                          8192 + 1024 * kk + 512 * fh + 512],
                                       start=(kk == 0), stop=False)
                        if qt == 0:
                            E.w("tensor", b2ev)
                            tok = E.op("tensor", "matmul", ps[:, 0:512],
                                       self.c_ones_rowb.ap()[0:1, :],
                                       self.b2row.ap()[0:1, 512 * fh:512 * fh + 512],
                                       start=False, stop=True)
                        else:
                            tok.k["stop"] = True
                        ev = E.inc("tensor", tok, "tv", 1)
                        self.ps_note_w(bank, ev)
                        last_y = ev
                        tmp = tmpb[:, 512 * (2 * ts4 + fh):512 * (2 * ts4 + fh) + 512]
                        if prevs is not None:
                            E.w("vector", prevs)
                            prevs = None
                        self.ps_r("vector", bank)
                        E.w("vector", self.g_ev)
                        tev = E.sig("vector", "tensor_scalar_mul", tmp,
                                    ps[:, 0:512],
                                    self.g_all.ap()[:, 8 * tc + e:8 * tc + e + 1])
                        self.ps_note_r(bank, tev)
                E.w("gpsimd", tev)
                row0 = b * T + 512 * tt
                dev = self.accum_dma(
                    ("y", b, tt),
                    self.y_add[row0:row0 + 512, :].rearrange(
                        "(s p) (f c) -> p s f c", p=128, c=512),
                    tmpb.rearrange("p (s f c) -> p s f c", f=2, c=512))
                self._ctmp_user[0] = dev
            self._wbuf_user[wkey] = last_y
            self._wbuf_user["b1c"] = last_h
            if qt == 3:
                self._wbuf_user["b2r"] = last_y

    def phase_D(self, b):
        E = self.E
        self.soft_join()
        self.ln_groups(b, self.inp["x"], self.y_add, self.x1_d, self.A1)
        self.dbg(f"xn1T_{b}", self.A1.ap(), wait=self.xnT_ev[-1])

    def phase_E(self, b):
        E = self.E
        self.soft_join()
        A1, A2, A3 = self.A1.ap(), self.A2.ap(), self.A3.ap()
        A4, A5, A6 = self.A4.ap(), self.A5.ap(), self.A6.ap()
        PS = [p.ap() for p in self.PS]
        xnT_ok = self.xnT_ev[-1]
        E.w("gpsimd", self._wbuf_user.get("E_A3"))
        E.dma("gpsimd", A3[:, 0:8192].rearrange("p (k d) -> p k d", d=1024),
              self.inp["l1wq"].rearrange("(k p) d -> p k d", p=128))
        E.dma("gpsimd", A3[:, 8192:10240].rearrange("p (k d) -> p k d", d=256),
              self.inp["l1wk"].rearrange("(k p) d -> p k d", p=128))
        wev = E.dma("gpsimd", A3[:, 10240:12288].rearrange("p (k d) -> p k d", d=256),
                    self.inp["l1wv"].rearrange("(k p) d -> p k d", p=128))
        lastq = None
        for m in range(8):
            for n in range(4):
                bank = (m * 4 + n) % 2
                ps = self.PS[bank].ap()
                self.ps_w(bank)
                E.w("tensor", wev, xnT_ok)
                for k in range(8):
                    tok = E.op("tensor", "matmul", ps[:, 0:512],
                               A3[:, 1024 * k + 128 * m:1024 * k + 128 * m + 128],
                               A1[:, 2048 * k + 512 * n:2048 * k + 512 * n + 512],
                               start=(k == 0), stop=(k == 7))
                ev = E.inc("tensor", tok, "tv", 1)
                self.ps_note_w(bank, ev)
                lastq = ev
                self.ps_r("scalar", bank)
                qev = E.sig("scalar", "activation",
                            out=A2[:, 2048 * m + 512 * n:2048 * m + 512 * n + 512],
                            in_=ps[:, 0:512], func=AF.Copy, scale=0.125)
                self.ps_note_r(bank, qev)
        kpre = self.WK.ap()[:, 0:4096]
        lastk = None
        for m in range(2):
            for n in range(4):
                bank = 2 + (m * 4 + n) % 2
                ps = self.PS[bank].ap()
                self.ps_w(bank)
                for k in range(8):
                    tok = E.op("tensor", "matmul", ps[:, 0:512],
                               A3[:, 8192 + 256 * k + 128 * m:8192 + 256 * k + 128 * m + 128],
                               A1[:, 2048 * k + 512 * n:2048 * k + 512 * n + 512],
                               start=(k == 0), stop=(k == 7))
                ev = E.inc("tensor", tok, "tv", 1)
                self.ps_note_w(bank, ev)
                lastk = ev
                self.ps_r("vector", bank)
                kev = E.sig("vector", "tensor_copy",
                            kpre[:, 2048 * m + 512 * n:2048 * m + 512 * n + 512],
                            ps[:, 0:512])
                self.ps_note_r(bank, kev)
        lastv = None
        for tc in range(16):
            bank = 4 + tc % 2
            ps = self.PS[bank].ap()
            self.ps_w(bank)
            for k in range(8):
                tok = E.op("tensor", "matmul", ps[:, 0:256],
                           A1[:, 2048 * k + 128 * tc:2048 * k + 128 * tc + 128],
                           A3[:, 10240 + 256 * k:10240 + 256 * k + 256],
                           start=(k == 0), stop=(k == 7))
            ev = E.inc("tensor", tok, "tv", 1)
            self.ps_note_w(bank, ev)
            lastv = ev
            self.ps_r("scalar", bank)
            for g4 in range(4):
                tok = E.op("scalar", "activation",
                           out=A5[:, 1040 * g4 + 65 * tc:1040 * g4 + 65 * tc + 64],
                           in_=ps[:, 64 * g4:64 * g4 + 64], func=AF.Copy)
            vev = E.inc("scalar", tok, "sv", 1)
            self.ps_note_r(bank, vev)
            self._vtok_ev = vev
        for g4 in range(4):
            va = A5[:, 1040 * g4:1040 * g4 + 1040]
            ev = E.sig("vector", "memset",
                       va.rearrange("p (c w) -> p c w", w=65)[:, :, 64:65], 1.0)
        self._vones_ev = ev
        sqb = A4[:, 0:2048]
        nq_tok = self.wk(4)[:, 0:256]
        k2_tok = self.wk(4)[:, 256:320]
        E.w("scalar", qev, kev)
        for tcg in range(8):
            prev = getattr(self, "_sqb_user", None)
            if prev is not None:
                E.w("scalar", prev)
            for k in range(8):
                tok = E.op("scalar", "activation", out=sqb[:, 256 * k:256 * k + 256],
                           in_=A2[:, 2048 * k + 256 * tcg:2048 * k + 256 * tcg + 256],
                           func=AF.Square)
            sev = E.inc("scalar", tok, "sv", 1)
            for half in range(2):
                tc = 2 * tcg + half
                bank = 6 + half
                ps = self.PS[bank].ap()
                self.ps_w(bank)
                E.w("tensor", sev)
                for k in range(8):
                    tok = E.op("tensor", "matmul", ps[:, 2 * k:2 * k + 2],
                               sqb[:, 256 * k + 128 * half:256 * k + 128 * half + 128],
                               self.c_hmap2.ap()[:],
                               start=(k == 0), stop=(k == 7))
                ev = E.inc("tensor", tok, "tv", 1)
                self.ps_note_w(bank, ev)
                self._sqb_user = ev
                self.ps_r("vector", bank)
                nev = E.sig("vector", "tensor_copy",
                            nq_tok[:, 16 * tc:16 * tc + 16], ps[:, 0:16])
                self.ps_note_r(bank, nev)
        for tc in range(16):
            bank = 6 + tc % 2
            ps = self.PS[bank].ap()
            prev = getattr(self, "_sqb_user", None)
            if prev is not None:
                E.w("scalar", prev)
            for c in range(2):
                tok = E.op("scalar", "activation", out=sqb[:, 256 * c:256 * c + 128],
                           in_=kpre[:, 2048 * c + 128 * tc:2048 * c + 128 * tc + 128],
                           func=AF.Square)
            sev = E.inc("scalar", tok, "sv", 1)
            self._sqk_ev = None
            self.ps_w(bank)
            E.w("tensor", sev)
            for c in range(2):
                tok = E.op("tensor", "matmul", ps[:, 32 + 2 * c:32 + 2 * c + 2],
                           sqb[:, 256 * c:256 * c + 128],
                           self.c_hmap2.ap()[:],
                           start=(c == 0), stop=(c == 1))
            ev = E.inc("tensor", tok, "tv", 1)
            self.ps_note_w(bank, ev)
            self._sqb_user = ev
            self.ps_r("vector", bank)
            nev = E.sig("vector", "tensor_copy",
                        k2_tok[:, 4 * tc:4 * tc + 4], ps[:, 32:36])
            self.ps_note_r(bank, nev)
        for tc in range(16):
            bank = 6 + tc % 2
            ps = self.PS[bank].ap()
            self.ps_w(bank)
            E.w("tensor", nev)
            tok = E.op("tensor", "transpose", ps[0:16, 64:192],
                       nq_tok[:, 16 * tc:16 * tc + 16], self.c_ident.ap()[:])
            tok = E.op("tensor", "transpose", ps[0:4, 192:320],
                       k2_tok[:, 4 * tc:4 * tc + 4], self.c_ident.ap()[:])
            ev = E.inc("tensor", tok, "tv", 1)
            self.ps_note_w(bank, ev)
            self.ps_r("vector", bank)
            E.sig("vector", "tensor_copy", self.nbq.ap()[:, 128 * tc:128 * tc + 128],
                  ps[0:16, 64:192])
            cev = E.sig("vector", "tensor_copy", self.nbk.ap()[:, 128 * tc:128 * tc + 128],
                        ps[0:4, 192:320])
            self.ps_note_r(bank, cev)
            self._nb_tr_ev = ev
        kmx = self.statB.ap()[0:4, 0:1]
        kmxev = E.sig("vector", "reduce_max", kmx, self.nbk.ap()[:], axis=AX.X)
        self.ps_w(6)
        E.w("tensor", kmxev)
        tok = E.op("tensor", "matmul", self.PS[6].ap()[0:16, 0:1],
                   self.c_rep4.ap()[:], kmx, start=True, stop=True)
        ev = E.inc("tensor", tok, "tv", 1)
        self.ps_note_w(6, ev)
        self.ps_r("vector", 6)
        k2ev = E.sig("vector", "tensor_copy", self.k2m.ap()[:],
                     self.PS[6].ap()[0:16, 0:1])
        self.ps_note_r(6, k2ev)
        mulv = E.sig("vector", "tensor_scalar_mul", self.nbq.ap()[:],
                     self.nbq.ap()[:], self.k2m.ap()[:])
        E.w("scalar", mulv)
        bev = E.sig("scalar", "activation", out=self.nbq.ap()[:],
                    in_=self.nbq.ap()[:], func=AF.Sqrt, scale=4096.0)
        E.w("vector", bev)
        nbv = E.sig("vector", "tensor_scalar_mul", self.nbq.ap()[:],
                    self.nbq.ap()[:], -1.0)
        E.w("gpsimd", nbv)
        self._negb_ev = E.dma("gpsimd", self.scr_nb, self.nbq.ap()[:])
        qsh = A6[:, 0:2048]
        t1b = A6[:, 2048:4096]
        rope_done_q = []
        for c in range(8):
            ch = A2[:, 2048 * c:2048 * c + 2048]
            prevs = [x for x in (getattr(self, "_qsh_user", None),)]
            E.w("sync", qev, *(p for p in prevs if p))
            sevs = []
            for blk, src_blk, in ((0, 1), (1, 0), (2, 3), (3, 2)):
                sevs.append(E.dma("sync", qsh[32 * blk:32 * blk + 32, :],
                                  ch[32 * src_blk:32 * src_blk + 32, :]))
            E.w("vector", *sevs)
            E.sig("vector", "tensor_tensor", t1b, qsh, self.c_sin.ap(), ALU.mult)
            E.sig("vector", "tensor_tensor", ch, ch, self.c_cos.ap(), ALU.mult)
            rev = E.sig("vector", "tensor_add", ch, ch, t1b)
            rope_done_q.append(rev)
            self._qsh_user = rev
        ksh = self.wk(4)[:, 0:1024]
        rope_done_k = []
        for c in range(2):
            ch = kpre[:, 2048 * c:2048 * c + 2048]
            for half in range(2):
                chh = ch[:, 1024 * half:1024 * half + 1024]
                E.w("sync", kev, self._nb_tr_ev, *rope_done_k[-1:])
                sevs = []
                for blk, src_blk in ((0, 1), (1, 0), (2, 3), (3, 2)):
                    sevs.append(E.dma("sync", ksh[32 * blk:32 * blk + 32, :],
                                      chh[32 * src_blk:32 * src_blk + 32, :]))
                E.w("vector", *sevs)
                cs = self.c_sin.ap()[:, 1024 * half:1024 * half + 1024]
                cc = self.c_cos.ap()[:, 1024 * half:1024 * half + 1024]
                t1f = self.wk(5)[:, 0:1024]
                E.sig("vector", "tensor_tensor", t1f, ksh, cs, ALU.mult)
                E.sig("vector", "tensor_tensor", chh, chh, cc, ALU.mult)
                rev = E.sig("vector", "tensor_add", chh, chh, t1f)
                rope_done_k.append(rev)
        self.dbg(f"vaug_{b}", A5[:, 0:4160], wait=self._vones_ev)
        self.dbg(f"qrope_{b}", A2[:, 0:16384], wait=self._qsh_user)
        self.dbg(f"krope_{b}", kpre, wait=rope_done_k[-1])
        self.dbg(f"vtok_{b}", A5[:, 0:4096], wait=self._vtok_ev)
        kaug_ev = []
        E.w("vector", *rope_done_k)
        E.w("vector", self._sqb_user)
        for g in range(4):
            c, rr = divmod(g, 2)
            dst = A4[:, 2048 * g:2048 * g + 2048]
            E.sig("vector", "tensor_copy", dst[0:64, :],
                  kpre[64 * rr:64 * rr + 64, 2048 * c:2048 * c + 2048])
            ev = E.sig("vector", "memset", dst[64:65, :], 1.0)
            kaug_ev.append(ev)
        st = self.statB.ap()
        for g in range(4):
            for hp in range(2):
                self.e_head_pair(b, g, 4 * g + 2 * hp, kaug_ev[g], rope_done_q)
        self.dbg(f"yT_{b}", A3[:, 0:16384], wait=self._yT_last)
        E.w("gpsimd", self._yT_last)
        woev = E.dma("gpsimd", A2[:, 0:8192].rearrange("p (k d) -> p k d", d=1024),
                     self.inp["l1wo"].rearrange("(k p) d -> p k d", p=128))
        self._wbuf_user["E_A3"] = self._yT_last
        E.w("vector", self._yT_last)
        tmpb = self.WK.ap()[:, 0:4096]
        prevs = self._ctmp_user.get(0)
        for tcg in range(4):
            tev = None
            for sub in range(4):
                tc = 4 * tcg + sub
                for fh in range(2):
                    bank = 6 + (2 * tc + fh) % 2
                    ps = self.PS[bank].ap()
                    self.ps_w(bank)
                    E.w("tensor", woev, self._yT_last)
                    for k in range(8):
                        tok = E.op("tensor", "matmul", ps[:, 0:512],
                                   A3[:, 2048 * k + 128 * tc:2048 * k + 128 * tc + 128],
                                   A2[:, 1024 * k + 512 * fh:1024 * k + 512 * fh + 512],
                                   start=(k == 0), stop=(k == 7))
                    ev = E.inc("tensor", tok, "tv", 1)
                    self.ps_note_w(bank, ev)
                    tmp = tmpb[:, 512 * (2 * sub + fh):512 * (2 * sub + fh) + 512]
                    if prevs is not None:
                        E.w("vector", prevs)
                        prevs = None
                    self.ps_r("vector", bank)
                    tev = E.sig("vector", "tensor_copy", tmp, ps[:, 0:512])
                    self.ps_note_r(bank, tev)
            E.w("gpsimd", tev)
            row0 = b * T + 512 * tcg
            dev = self.accum_dma(
                ("y2", b, tcg),
                self.y2_add[row0:row0 + 512, :].rearrange(
                    "(s p) (f c) -> p s f c", p=128, c=512),
                tmpb.rearrange("p (s f c) -> p s f c", f=2, c=512), first=True)
            prevs = dev
            self._ctmp_user[0] = dev

    def e_head_pair(self, b, g, h0, kaug_ev, rope_done_q):
        E = self.E
        A2, A3 = self.A2.ap(), self.A3.ap()
        A4, A5, A6 = self.A4.ap(), self.A5.ap(), self.A6.ap()
        PS = [p.ap() for p in self.PS]
        kaug = A4[:, 2048 * g:2048 * g + 2048]
        heads = (h0, h0 + 1)
        qaug = {}
        brow = {}
        qb = {}
        for i, h in enumerate(heads):
            qa = A6[:, 2048 * i:2048 * i + 2048]
            qaug[h] = qa
            prev = self._qaug_user.get(i)
            if prev is not None:
                E.w("vector", prev)
            E.w("vector", rope_done_q[h // 2], kaug_ev)
            qb[h] = E.sig("vector", "tensor_copy", qa[0:64, :],
                          A2[64 * (h % 2):64 * (h % 2) + 64,
                             2048 * (h // 2):2048 * (h // 2) + 2048])
            if prev is not None:
                E.w("gpsimd", prev)
            E.w("gpsimd", self._negb_ev, rope_done_q[-1])
            brow[h] = E.dma("gpsimd", qa[64:65, :], self.scr_nb[h:h + 1, :])
        PTB = {heads[0]: (2, 3), heads[1]: (6, 7)}
        AVB = {heads[0]: 4, heads[1]: 5}
        BCB = {heads[0]: 0, heads[1]: 1}
        for qg in range(4):
            njt = 4 * qg + 4
            for h in heads:
                self.ps_w(AVB[h])
            pend = {}
            for jt in range(njt + 1):
                if jt < njt:
                    for i, h in enumerate(heads):
                        bank = PTB[h][jt % 2]
                        ps = PS[bank]
                        self.ps_w(bank)
                        E.w("tensor", brow[h], qb[h], self._vones_ev, self._vtok_ev)
                        tok = E.op("tensor", "matmul", ps[:, 0:512],
                                   kaug[0:65, 128 * jt:128 * jt + 128],
                                   qaug[h][0:65, 512 * qg:512 * qg + 512],
                                   start=True, stop=True)
                        ev = E.inc("tensor", tok, "tv", 1)
                        self.ps_note_w(bank, ev)
                        if jt >= 4 * qg:
                            self.ps_r("vector", bank)
                            mev = E.sig("vector", "tensor_tensor", ps[:, 0:512],
                                        ps[:, 0:512],
                                        self.c_mask_j.ap()[:, 512 * (jt - 4 * qg):
                                                           512 * (jt - 4 * qg) + 512],
                                        ALU.add)
                            self._ps_wv[bank] = mev
                        slot = 2 * i + jt % 2
                        ptb = self.ptbuf.ap()[:, 512 * slot:512 * slot + 512]
                        prev2 = self._pt_user.get(slot)
                        if prev2 is not None:
                            E.w("scalar", prev2)
                        self.ps_r("scalar", bank)
                        pev = E.sig("scalar", "activation", out=ptb,
                                    in_=ps[:, 0:512], func=AF.Exp)
                        self.ps_note_r(bank, pev)
                        pend[(i, jt)] = (pev, ptb)
                if jt >= 1:
                    jp = jt - 1
                    for i, h in enumerate(heads):
                        pev, ptb = pend.pop((i, jp))
                        E.w("tensor", pev)
                        tok = E.op("tensor", "matmul", PS[AVB[h]][0:65, 0:512],
                                   A5[:, 1040 * g + 65 * jp:1040 * g + 65 * jp + 65],
                                   ptb, start=(jp == 0), stop=(jp == njt - 1))
                        avev = E.inc("tensor", tok, "tv", 1)
                        self.ps_note_w(AVB[h], avev)
                        self._pt_user[2 * i + jp % 2] = avev
            for i, h in enumerate(heads):
                av = AVB[h]
                rs = self.rs_row.ap()[0:1, 512 * i:512 * i + 512]
                self.ps_r("vector", av)
                prev2 = self._rs_user2.get(i)
                if prev2 is not None:
                    E.w("vector", prev2)
                rev = E.sig("vector", "reciprocal", rs, PS[av][64:65, 0:512])
                ysb = self.WK.ap()[64 * i:64 * i + 64, 5 * 1024:5 * 1024 + 512]
                yev0 = E.sig("vector", "tensor_copy", ysb, PS[av][0:64, 0:512])
                self.ps_note_r(av, yev0)
                bc = BCB[h]
                self.ps_w(bc)
                E.w("tensor", rev)
                tok = E.op("tensor", "matmul", PS[bc][0:64, 0:512],
                           self.c_ones_rowf.ap()[0:1, 0:64], rs,
                           start=True, stop=True)
                bcev = E.inc("tensor", tok, "tv", 1)
                self.ps_note_w(bc, bcev)
                self._rs_user2[i] = bcev
                self.ps_r("vector", bc)
                E.w("vector", bcev)
                yv = E.sig("vector", "tensor_tensor",
                           A3[64 * (h % 2):64 * (h % 2) + 64,
                              2048 * (h // 2) + 512 * qg:2048 * (h // 2) + 512 * qg + 512],
                           ysb, PS[bc][0:64, 0:512], ALU.mult)
                self.ps_note_r(bc, yv)
                self._yT_last = yv
        self._qaug_user[0] = ("tv", E.cnt["tv"])
        self._qaug_user[1] = ("tv", E.cnt["tv"])

    def phase_F(self, b):
        E = self.E
        self.soft_join()
        A1, A2, A5 = self.A1.ap(), self.A2.ap(), self.A5.ap()
        self.gates(b, 1)
        fcv = self.inp["wfcT"].rearrange("e (k p) i -> e p k i", p=128)
        pjv = self.inp["wprojT"].rearrange("e (k p) d -> e p k d", p=128)

        def load_fe(e):
            par = e % 2
            fc0 = 4096 * par
            pj0 = 8192 + 4096 * par
            prev = self._wbuf_user.get(f"F_{par}")
            if prev is not None:
                E.w("gpsimd", prev)
            E.dma("gpsimd", A2[:, fc0:fc0 + 4096].rearrange("p (k d) -> p k d", d=512),
                  fcv[e])
            self._fld[e] = E.dma("gpsimd",
                                 A2[:, pj0:pj0 + 4096].rearrange("p (k d) -> p k d", d=1024),
                                 pjv[e])

        self._fld = {}
        self._fh_user = {}
        load_fe(0)
        for e in range(E1):
            par = e % 2
            fc0 = 4096 * par
            pj0 = 8192 + 4096 * par
            wkey = f"F_{par}"
            wev = self._fld[e]
            if e + 1 < E1:
                load_fe(e + 1)
            last_y = None
            for tt in range(4):
                h0 = 2048 * (tt % 2)
                for m in range(4):
                    bank = m % 2
                    ps = self.PS[bank].ap()
                    self.ps_w(bank)
                    E.w("tensor", wev, self.xnT_ev[-1])
                    for k in range(8):
                        tok = E.op("tensor", "matmul", ps[:, 0:512],
                                   A2[:, fc0 + 512 * k + 128 * m:fc0 + 512 * k + 128 * m + 128],
                                   A1[:, 2048 * k + 512 * tt:2048 * k + 512 * tt + 512],
                                   start=(k == 0), stop=(k == 7))
                    ev = E.inc("tensor", tok, "tv", 1)
                    self.ps_note_w(bank, ev)
                    self.ps_r("scalar", bank)
                    prevh = self._fh_user.get(tt % 2)
                    if prevh is not None:
                        E.w("scalar", prevh)
                    hev = E.sig("scalar", "activation",
                                out=A5[:, h0 + 512 * m:h0 + 512 * m + 512],
                                in_=ps[:, 0:512], func=AF.Gelu)
                    self.ps_note_r(bank, hev)
                tmpb = self.WK.ap()[:, 0:4096]
                prevs = self._ctmp_user.get(0)
                tev = None
                for ts4 in range(4):
                    tc = 4 * tt + ts4
                    for fh in range(2):
                        bank = 2 + (2 * ts4 + fh) % 4
                        ps = self.PS[bank].ap()
                        self.ps_w(bank)
                        E.w("tensor", hev)
                        for kk in range(4):
                            tok = E.op("tensor", "matmul", ps[:, 0:512],
                                       A5[:, h0 + 512 * kk + 128 * ts4:h0 + 512 * kk + 128 * ts4 + 128],
                                       A2[:, pj0 + 1024 * kk + 512 * fh:
                                          pj0 + 1024 * kk + 512 * fh + 512],
                                       start=(kk == 0), stop=(kk == 3))
                        ev = E.inc("tensor", tok, "tv", 1)
                        self.ps_note_w(bank, ev)
                        last_y = ev
                        self._fh_user[tt % 2] = ev
                        tmp = tmpb[:, 512 * (2 * ts4 + fh):512 * (2 * ts4 + fh) + 512]
                        if prevs is not None:
                            E.w("vector", prevs)
                            prevs = None
                        self.ps_r("vector", bank)
                        E.w("vector", self.g_ev)
                        tev = E.sig("vector", "tensor_scalar_mul", tmp,
                                    ps[:, 0:512],
                                    self.g_all.ap()[:, 8 * tc + e:8 * tc + e + 1])
                        self.ps_note_r(bank, tev)
                E.w("gpsimd", tev)
                row0 = b * T + 512 * tt
                dev = self.accum_dma(
                    ("y2", b, tt),
                    self.y2_add[row0:row0 + 512, :].rearrange(
                        "(s p) (f c) -> p s f c", p=128, c=512),
                    tmpb.rearrange("p (s f c) -> p s f c", f=2, c=512))
                self._ctmp_user[0] = dev
            self._wbuf_user[wkey] = last_y

    def phase_G(self, b):
        E = self.E
        self.soft_join()
        self.dbg(f"yadd_{b}", self.y_add[b * T:(b + 1) * T, :])
        self.dbg(f"y2add_{b}", self.y2_add[b * T:(b + 1) * T, :])
        self.dbg(f"x1d_{b}", self.x1_d[b * T:(b + 1) * T, :])
        xb = self.A3.ap()[:, 0:8192].bitcast(F32)
        yb = self.A2.ap()[:, 0:8192].bitcast(F32)
        for tg in range(4):
            row0 = b * T + 512 * tg
            pv = getattr(self, "_g_user", None)
            if pv is not None:
                E.w("sync", pv)
            ev_x = E.dma("sync", xb.rearrange("p (s c) -> p s c", c=1024),
                         self.x1_d[row0:row0 + 512, :].rearrange(
                             "(s p) c -> p s c", p=128))
            ev_y = E.dma("sync", yb.rearrange("p (s c) -> p s c", c=1024),
                         self.y2_add[row0:row0 + 512, :].rearrange(
                             "(s p) c -> p s c", p=128))
            E.w("vector", ev_x, ev_y)
            aev = E.sig("vector", "tensor_add", xb, xb, yb)
            E.w("sync", aev)
            oev = E.dma("sync", self.out[row0:row0 + 512, :].rearrange(
                "(s p) c -> p s c", p=128),
                xb.rearrange("p (s c) -> p s c", c=1024))
            self._g_user = oev

    def finalize(self):
        E = self.E
        for s in ("sq", "gq"):
            if E.cnt[s]:
                E.w("sync", (s, E.cnt[s]))
        for ev in self.dbg_outs:
            E.w("sync", ev)

    def build(self):
        import contextlib
        with contextlib.ExitStack() as es:
            sems = {}
            for s in SEMS:
                sems[s] = es.enter_context(self.nc.semaphore(s))
            self.E.replay(self.nc, sems)
        return self.nc


def host_prepare(inp):
    g = lambda k: np.asarray(inp[k], dtype=np.float32)
    out = {}
    out["wq"] = np.ascontiguousarray(g("L0_attn_Wq").T)
    out["wk"] = np.ascontiguousarray(g("L0_attn_Wk").T)
    out["wv"] = np.ascontiguousarray(g("L0_attn_Wv").T)
    out["wbeta"] = np.ascontiguousarray(g("L0_attn_Wbeta").T)
    out["walpha"] = np.ascontiguousarray(g("L0_attn_Walpha").T)
    out["bq_c"] = np.ascontiguousarray(g("L0_attn_bq").reshape(8, 128).T)
    out["balpha_c"] = np.ascontiguousarray(g("L0_attn_balpha").reshape(8, 128).T)
    out["bk_r"] = g("L0_attn_bk").reshape(1, D)
    out["bv_r"] = g("L0_attn_bv").reshape(1, D)
    out["bbeta_r"] = g("L0_attn_bbeta").reshape(1, D)
    out["w1"] = g("L0_ffn_W1")
    out["w2"] = g("L0_ffn_W2")
    b1 = g("L0_ffn_b1")
    out["b1c"] = np.ascontiguousarray(
        b1.reshape(E0, 4, 8, 128).transpose(0, 1, 3, 2)).reshape(E0 * 4 * 128, 8)
    out["b2r"] = g("L0_ffn_b2")
    out["wr"] = g("L0_ffn_Wr")
    out["br_r"] = g("L0_ffn_br").reshape(1, E0)
    out["l1wq"] = g("L1_attn_Wq")
    out["l1wk"] = g("L1_attn_Wk")
    out["l1wv"] = g("L1_attn_Wv")
    out["l1wo"] = g("L1_attn_Wo")
    out["wgT"] = np.ascontiguousarray(g("L1_ffn_w_gate").T)
    out["wfcT"] = np.ascontiguousarray(g("L1_ffn_w_fc").transpose(0, 2, 1))
    out["wprojT"] = np.ascontiguousarray(g("L1_ffn_w_proj").transpose(0, 2, 1))
    out["ident"] = np.eye(128, dtype=np.float32)
    out["ones_col"] = np.ones((128, 1), np.float32)
    out["ones_row"] = np.ones((1, 128), np.float32)
    out["umat"] = np.triu(np.ones((128, 128), np.float32))
    c1 = np.zeros((128, 128), np.float32)
    c1[0, :] = 1.0
    for k in range(1, 128):
        c1[k, k] = 1.0
    out["c1mat"] = c1
    r = np.zeros((128, 128), np.float32)
    for k in range(1, 128):
        r[k, :k] = NEG
    out["rconst"] = np.ascontiguousarray(np.tile(r, (1, H)))
    hm = np.zeros((128, 2), np.float32)
    hm[0:64, 0] = 1.0 / 64
    hm[64:128, 1] = 1.0 / 64
    out["hmap2"] = hm
    half = HD // 2
    inv = 1.0 / (10000.0 ** (np.arange(half, dtype=np.float64) / half))
    tpos = np.arange(T, dtype=np.float64)
    fr = np.outer(tpos, inv)
    emb = np.concatenate([fr, fr], axis=1)
    cos = np.cos(emb).T.astype(np.float32)
    sin = np.sin(emb).T.astype(np.float32)
    sin[:half, :] *= -1.0
    out["rope_cos"] = np.ascontiguousarray(np.tile(cos, (2, 1)))
    out["rope_sin"] = np.ascontiguousarray(np.tile(sin, (2, 1)))
    mj = np.zeros((128, 4, 512), np.float32)
    for rr in range(4):
        for jj in range(128):
            mj[jj, rr, : min(512, 128 * rr + jj)] = NEG
    out["mask_j"] = np.ascontiguousarray(mj.reshape(128, 4 * 512))
    mq = np.zeros((128, 4, 512), np.float32)
    for rr in range(4):
        for qq in range(128):
            lo = 128 * rr + qq + 1
            if lo < 512:
                mq[qq, rr, lo:] = NEG
    out["mask_q"] = np.ascontiguousarray(mq.reshape(128, 4 * 512))
    rep4 = np.zeros((4, 16), np.float32)
    for gg in range(4):
        rep4[gg, 4 * gg:4 * gg + 4] = 1.0
    out["rep4"] = rep4
    return out


_BUILD_OVERRIDE = {}


def kernel(**inputs):
    shared = host_prepare(inputs)
    x = np.asarray(inputs["L0_attn_x"], dtype=np.float32)
    p = Prog(**_BUILD_OVERRIDE)
    build_all(p)
    p.finalize()
    nc = p.build()
    in_maps = []
    for c in range(NCORES):
        m = dict(shared)
        m["x"] = np.ascontiguousarray(x[B * c: B * (c + 1)].reshape(B * T, D))
        in_maps.append(m)
    res = run_bass_kernel_spmd(nc, in_maps, core_ids=list(range(NCORES)))
    out = np.concatenate(
        [r["out"].reshape(B, T, D) for r in res.results], axis=0)
    return out


def build_all(p):
    p.load_consts()
    for b in p.batches:
        if "A" in p.phases:
            p.phase_A(b)
        if "B" in p.phases:
            p.phase_B(b)
        if "C" in p.phases:
            p.phase_C(b)
        if "D" in p.phases:
            p.phase_D(b)
        if "E" in p.phases:
            p.phase_E(b)
        if "F" in p.phases:
            p.phase_F(b)
        if "G" in p.phases:
            p.phase_G(b)
